# version 05
# baseline (speedup 1.0000x reference)
import contextlib
import numpy as np
import ml_dtypes
import concourse.bass as bass
import concourse.mybir as mybir
from concourse.bass_utils import run_bass_kernel_spmd

F32 = mybir.dt.float32
BF16 = mybir.dt.bfloat16
AF = mybir.ActivationFunctionType
ALU = mybir.AluOpType
AX = mybir.AxisListType.X

L, D = 4096, 1024
NCH = 8
TILE = 512
NTILE = L // TILE
E, FF = 8, 4096
TB = 1024
NTB = L // TB

import os
FULL_ATTN = os.environ.get("KBASS_MINI", "") != "1"
DEBUG_DUMP = os.environ.get("KBASS_DEBUG", "") == "1"


class Tok:
    __slots__ = ("sem", "val", "eng")

    def __init__(self, sem, val, eng):
        self.sem, self.val, self.eng = sem, val, eng


class Buf:
    __slots__ = ("h", "w", "readers", "name")

    def __init__(self, h, name=""):
        self.h, self.name = h, name
        self.w = None
        self.readers = {}


class Prog:
    ENGS = ("t", "v", "s", "g", "y")
    ESEM = {"t": "pe", "v": "dv", "s": "ac", "g": "po", "y": "sy"}

    def __init__(self):
        self.ops = {e: [] for e in self.ENGS}
        self.counts = {}
        self.hw = {e: {} for e in self.ENGS}
        self.sems = {}

    def _wait(self, eng, tok):
        if tok is None:
            return
        if tok.eng == eng and eng == "t":
            return
        if self.hw[eng].get(tok.sem, 0) >= tok.val:
            return
        self.hw[eng][tok.sem] = tok.val
        s, v = tok.sem, tok.val
        self.ops[eng].append(lambda e: e.wait_ge(self.sems[s], v))

    def emit(self, eng, fns, reads=(), writes=(), sem=None, dma=False,
             extra_waits=()):
        if callable(fns):
            fns = [fns]
        for tok in extra_waits:
            self._wait(eng, tok)
        for b in reads:
            self._wait(eng, b.w)
        for b in writes:
            self._wait(eng, b.w)
            for tok in list(b.readers.values()):
                self._wait(eng, tok)
        tok = None
        amt = 16 if dma else 1
        n = len(fns)
        if sem is not None and not dma and eng != "t" and n > 1:
            for i, f in enumerate(fns):
                self.counts[sem] = self.counts.get(sem, 0) + 1
                v = self.counts[sem]
                self.ops[eng].append(
                    lambda e, f=f, s=sem: f(e).then_inc(self.sems[s], 1))
                if i < n - 1:
                    self.ops[eng].append(
                        lambda e, s=sem, v=v: e.wait_ge(self.sems[s], v))
            tok = Tok(sem, self.counts[sem], eng)
            self.hw[eng][sem] = self.counts[sem] - 1
        else:
            if sem is not None:
                self.counts[sem] = self.counts.get(sem, 0) + amt
                tok = Tok(sem, self.counts[sem], None if dma else eng)
            for i, f in enumerate(fns):
                if sem is not None and i == n - 1:
                    self.ops[eng].append(
                        lambda e, f=f, s=sem, a=amt: f(e).then_inc(self.sems[s], a))
                else:
                    self.ops[eng].append(f)
        if tok is not None:
            for b in reads:
                old = b.readers.get(tok.sem)
                if old is None or old.val < tok.val:
                    b.readers[tok.sem] = tok
            for b in writes:
                b.w = tok
                b.readers = {}
        return tok

    def barrier(self):
        for eng in self.ENGS:
            for sem in sorted(self.counts.keys()):
                tot = self.counts.get(sem, 0)
                if tot > self.hw[eng].get(sem, 0):
                    self.hw[eng][sem] = tot
                    self.ops[eng].append(
                        lambda e, s=sem, v=tot: e.wait_ge(self.sems[s], v))


def _bf(x):
    return np.ascontiguousarray(np.asarray(x, np.float32).astype(ml_dtypes.bfloat16))


def build_consts():
    c = {}
    c["ident_bf"] = _bf(np.eye(128))
    c["ident_f32"] = np.ascontiguousarray(np.eye(128, dtype=np.float32))
    c["ones_1x128"] = _bf(np.ones((1, 128)))
    c["ones_128x1"] = _bf(np.ones((128, 1)))
    selh = np.zeros((4, 512), np.float32)
    for h in range(4):
        selh[h, h * 128:(h + 1) * 128] = 1.0
    c["selh"] = _bf(selh)
    sele = np.zeros((8, 1024), np.float32)
    for e in range(8):
        sele[e, e * 128:(e + 1) * 128] = 1.0
    c["sele"] = _bf(sele)
    mU = np.triu(np.ones((128, 128), np.float32), 1)
    c["maskUneg"] = _bf(-mU)
    c["maskLneg"] = _bf(-mU.T)
    c["maskUincl"] = _bf(np.triu(np.ones((128, 128), np.float32), 0))
    sred = np.zeros((3, 8, 128, 12), np.float32)
    for s in range(3):
        for j in range(8):
            sred[s, j, :, s * 4 + j // 2] = 1.0 / 256.0
    c["statred"] = _bf(sred.reshape(24, 128, 12))
    rred = np.zeros((8, 128, 4), np.float32)
    for j in range(8):
        rred[j, :, j // 2] = 1.0 / 256.0
    c["rmsred"] = _bf(rred)
    l2r_ = np.zeros((8, 128, 4), np.float32)
    for j in range(8):
        l2r_[j, :, j // 2] = 1.0
    c["l2red"] = _bf(l2r_)
    return c


def build_smalls(inp):
    cols = []
    meta = {}

    def add(name, arr):
        arr = np.ascontiguousarray(np.asarray(arr, np.float32))
        assert arr.shape[0] == 128, (name, arr.shape)
        meta[name] = (sum(a.shape[1] for a in cols), arr.shape[1])
        cols.append(arr)

    for nm in ("conv_q", "conv_k", "conv_v"):
        w = np.asarray(inp[f"L0_attn_{nm}"])
        add(nm, w.reshape(8, 128, 4).transpose(1, 0, 2).reshape(128, 32))
    fsf = np.asarray(inp["L0_attn_fir_short"]).reshape(1024, 3)
    add("fir_s", fsf.reshape(8, 128, 3).transpose(1, 0, 2).reshape(128, 24))
    flf = np.asarray(inp["L0_attn_fir_long"]).reshape(1024, 31)
    add("fir_l", flf.reshape(8, 128, 31).transpose(1, 0, 2).reshape(128, 248))
    add("o_norm", np.asarray(inp["L0_attn_o_norm_w"]).reshape(2, 128).T)
    add("br1", np.asarray(inp["L0_attn_br1"]).reshape(16, 128).T)

    def pad4(v):
        z = np.zeros((128, 1), np.float32)
        z[:4, 0] = np.asarray(v, np.float32)
        return z

    add("alpha_id", pad4(inp["L0_attn_alpha_id"]))
    add("bid", pad4(inp["L0_attn_bid"]))
    add("log_tau", pad4(np.asarray(inp["L0_attn_log_tau_group"])[[0, 0, 1, 1]]))
    br2 = np.asarray(inp["L0_attn_br2"])
    for cc in range(3):
        add(f"br2_{cc}", pad4(br2[cc::3]))
    return np.ascontiguousarray(np.concatenate(cols, 1)), meta


def build_nc(consts, smeta, smalls_shape):
    nc = bass.Bass("TRN2", target_bir_lowering=False)
    st = contextlib.ExitStack()
    p = Prog()

    def din(name, shape, dtype=F32):
        return nc.dram_tensor(name, list(shape), dtype, kind="ExternalInput")

    x_in = din("x", (L, D))
    wq_d = din("wq", (D, D)); wk_d = din("wk", (D, D)); wv_d = din("wv", (D, D))
    wb_d = din("wb", (D, 4)); wid_d = din("wid", (D, 4)); wo_d = din("wo", (D, D))
    wr1x_d = din("wr1x", (D, 2048))
    wr1m_d = din("wr1m", (12, 2048)); wr1s_d = din("wr1s", (12, 2048))
    wr2c_d = [din(f"wr2c{i}", (2048, 4)) for i in range(3)]
    wred_d = din("wred", (D, 16)); wg_d = din("wg", (8, 16))
    w1_d = din("w1", (E, D, FF)); w2_d = din("w2", (E, FF, D))
    b1_d = din("b1r", (E, 128, 32)); b2_d = din("b2r", (E, 128, 8))
    smalls_d = din("smalls", smalls_shape)

    out_d = nc.dram_tensor("out", [L, D], F32, kind="ExternalOutput")
    x1s = nc.dram_tensor("x1s", [L, D], F32, kind="Internal")
    xn1s = nc.dram_tensor("xn1s", [D, L], BF16, kind="Internal")
    dbg = {}
    if DEBUG_DUMP:
        for nm, shp, dt_ in (("q", (D, L), BF16), ("k", (D, L), BF16),
                             ("v", (D, L), BF16), ("fs", (D, L), BF16),
                             ("fl", (D, L), BF16), ("delta", (D, L), BF16),
                             ("beta", (4, L), F32), ("idsc", (4, L), BF16),
                             ("statm", (12, L), F32), ("stats", (12, L), F32),
                             ("pp", (12, L), BF16), ("ocomb", (D, L), BF16),
                             ("on", (D, L), BF16), ("xn", (D, L), BF16)):
            dbg[nm] = nc.dram_tensor("dbg_" + nm, list(shp), dt_, kind="Internal")
        for nm, shp, dt_ in (("d_qn", (128, 256), BF16), ("d_kn", (128, 256), BF16),
                             ("d_kb", (128, 512), BF16), ("d_r", (128, 128), BF16),
                             ("d_u2", (128, 512), BF16), ("d_wfm", (128, 512), BF16),
                             ("d_att", (128, 128), BF16), ("d_otm", (128, 256), BF16),
                             ("d_at", (128, 128), BF16), ("d_S", (128, 512), BF16),
                             ("d_vbt", (128, 256), BF16), ("d_knt", (128, 256), BF16)):
            dbg[nm] = nc.dram_tensor("dbg_" + nm, list(shp), dt_, kind="Internal")

    cT = {k: nc.inline_tensor(v, f"c_{k}") for k, v in consts.items()}

    def sb(name, shape, dtype, stack=st):
        return Buf(stack.enter_context(
            nc.sbuf_tensor("sb_" + name, list(shape), dtype)), name)

    def psum(name, shape, dtype):
        return Buf(st.enter_context(nc.psum_tensor(name, list(shape), dtype)),
                   name)

    ident_bf = sb("ident_bf", (128, 128), BF16)
    ident_f32 = sb("ident_f32", (128, 128), F32)
    ones1x = sb("ones1x", (1, 128), BF16)
    ones_c = sb("ones_c", (128, 1), BF16)
    selh = sb("selh", (4, 512), BF16)
    sele = sb("sele", (8, 1024), BF16)
    maskUneg = sb("maskUneg", (128, 128), BF16)
    maskLneg = sb("maskLneg", (128, 128), BF16)
    maskUincl = sb("maskUincl", (128, 128), BF16)
    statred = sb("statred", (128, 24 * 12), BF16)
    rmsred = sb("rmsred", (128, 32), BF16)
    l2red = sb("l2red", (128, 32), BF16)
    smalls = sb("smalls", smalls_shape, F32)
    sal = sb("sal", (4, 1), F32)
    invt = sb("invt", (4, 1), F32)
    epsc = sb("epsc", (128, 2), F32)

    PF = [psum(f"pf{i}", (128, 512), F32) for i in range(6)]
    PB = psum("pb", (128, 1024), BF16)
    PB2 = psum("pb2", (128, 1024), BF16)

    def scol(name, j0=0, n=None):
        base, width = smeta[name]
        if n is None:
            n = width
        return smalls.h[:, base + j0: base + j0 + n]

    _setup_bufs = []
    for bufb, key in ((ident_bf, "ident_bf"), (ident_f32, "ident_f32"),
                      (ones1x, "ones_1x128"), (ones_c, "ones_128x1"),
                      (selh, "selh"), (sele, "sele"), (maskUneg, "maskUneg"),
                      (maskLneg, "maskLneg"), (maskUincl, "maskUincl")):
        p.emit("y", lambda e, d=bufb.h, s=cT[key]: e.dma_start(d[:], s.ap()),
               writes=[bufb], sem="ds1", dma=True)
        _setup_bufs.append(bufb)
    p.emit("y", lambda e: e.dma_start(
        statred.h.rearrange("p (a m) -> p a m", a=24),
        cT["statred"].rearrange("a p m -> p a m")),
        writes=[statred], sem="ds1", dma=True)
    p.emit("y", lambda e: e.dma_start(
        rmsred.h.rearrange("p (a m) -> p a m", a=8),
        cT["rmsred"].rearrange("a p m -> p a m")),
        writes=[rmsred], sem="ds1", dma=True)
    p.emit("y", lambda e: e.dma_start(
        l2red.h.rearrange("p (a m) -> p a m", a=8),
        cT["l2red"].rearrange("a p m -> p a m")),
        writes=[l2red], sem="ds1", dma=True)
    _setup_bufs.append(l2red)
    p.emit("y", lambda e: e.dma_start(smalls.h[:], smalls_d.ap()),
           writes=[smalls], sem="ds1", dma=True)
    _setup_bufs += [statred, rmsred, smalls]
    _t0 = Tok("ds1", p.counts["ds1"], None)
    for _b in _setup_bufs:
        _b.w = _t0
    _setup_bufs = []
    p.emit("v", [
        lambda e: e.memset(epsc.h[:, 0:1], 1e-5),
        lambda e: e.memset(epsc.h[:, 1:2], 1e-6),
    ], writes=[epsc], sem="dv")
    p.emit("s", lambda e: e.activation(sal.h[:], scol("alpha_id")[0:4, :],
                                       AF.Sigmoid),
           reads=[smalls], writes=[sal], sem="ac")
    p.emit("s", lambda e: e.activation(invt.h[:], scol("log_tau")[0:4, :],
                                       AF.Exp, scale=-1.0),
           reads=[smalls], writes=[invt], sem="ac")

    rot = [0]

    def pnext(pool=(2, 3, 4, 5)):
        b = PF[pool[rot[0] % len(pool)]]
        rot[0] += 1
        return b

    def mmgrp(bank, out_ap, items, reads, eng_extra=()):
        n = len(items)
        fns = [(lambda e, l=l, r=r, s0=(i == 0), s1=(i == n - 1):
                e.matmul(out_ap, l, r, start=s0, stop=s1))
               for i, (l, r) in enumerate(items)]
        return p.emit("t", fns, reads=list(reads), writes=[bank], sem="pe",
                      extra_waits=eng_extra)

    ph1 = contextlib.ExitStack()
    x_sub = [sb(f"x_sub{i}", (128, 1024), F32, ph1) for i in range(2)]
    xn_tm = sb("xn_tm", (128, 2048), BF16, ph1)
    xn_fm2 = [sb(f"xn_fm{i}", (128, 8 * 512), BF16, ph1) for i in range(2)]
    ln_sc = sb("ln_sc", (128, 32), F32, ph1)
    sqln = sb("sqln", (128, 1024), BF16, ph1)

    def ln_norm(src, scol0, dst, dcol0, sidx):
        xap = src.h[:, scol0:scol0 + 1024]
        nap = dst.h[:, dcol0:dcol0 + 1024]
        su = ln_sc.h[:, sidx * 4: sidx * 4 + 1]
        sq = ln_sc.h[:, sidx * 4 + 1: sidx * 4 + 2]
        mm = ln_sc.h[:, sidx * 4 + 2: sidx * 4 + 3]
        rr = ln_sc.h[:, sidx * 4 + 3: sidx * 4 + 4]
        p.emit("v", lambda e: e.reduce_sum(su, xap, axis=AX),
               reads=[src], writes=[ln_sc], sem="dv")
        p.emit("s", lambda e: e.activation(sqln.h[:], xap, AF.Square,
                                           accum_out=sq),
               reads=[src], writes=[ln_sc, sqln], sem="ac")
        p.emit("v", [
            lambda e: e.tensor_single_scalar(mm, su, 1.0 / 1024, ALU.mult),
            lambda e: e.tensor_single_scalar(sq, sq, 1.0 / 1024, ALU.mult),
            lambda e: e.tensor_tensor(rr, mm, mm, ALU.mult),
            lambda e: e.tensor_tensor(sq, sq, rr, ALU.subtract),
        ], reads=[ln_sc], writes=[ln_sc], sem="dv")
        p.emit("s", lambda e, sq=sq: e.activation(sq, sq, AF.Sqrt,
               bias=epsc.h[:, 0:1]),
               reads=[ln_sc, epsc], writes=[ln_sc], sem="ac")
        p.emit("v", [
            lambda e: e.reciprocal(rr, sq),
            lambda e: e.tensor_scalar(nap, xap, mm, rr, ALU.subtract, ALU.mult),
        ], reads=[src, ln_sc], writes=[dst, ln_sc], sem="dv")

    def tm_to_fm_sub(src, scol0, dst, tok_in_tile, n_sub_tok=128):
        fns = []
        for k in range(NCH):
            inap = src.h[:, scol0 + k * 128: scol0 + (k + 1) * 128]
            outap = PB.h[:, k * 128:(k + 1) * 128]
            fns.append(lambda e, i=inap, o=outap: e.transpose(o, i, ident_bf.h[:]))
        p.emit("t", fns, reads=[src, ident_bf], writes=[PB], sem="pe")
        dv = dst.h.rearrange("p (k c) -> p k c", k=NCH)[
            :, :, tok_in_tile:tok_in_tile + 128]
        sv = PB.h.rearrange("p (k c) -> p k c", k=NCH)
        p.emit("v", lambda e: e.tensor_copy(dv, sv),
               reads=[PB], writes=[dst], sem="dv")

    if FULL_ATTN:
        zping = [sb(f"zping{i}", (128, 515), BF16, ph1) for i in range(2)]
        zhalo = {nm: sb(f"zhalo_{nm}", (128, 24), BF16, ph1)
                 for nm in ("q", "k", "v")}
        q_f = sb("q_f", (128, 8 * 512), BF16, ph1)
        k_f = sb("k_f", (128, 8 * 512), BF16, ph1)
        v_f = sb("v_f", (128, 8 * 542), BF16, ph1)
        cvp = [sb(f"cvp{i}", (128, 512), BF16, ph1) for i in range(2)]
        cvg = [sb(f"cvg{i}", (128, 512), BF16, ph1) for i in range(2)]
        pca = [sb(f"pca{i}", (128, 512), BF16, ph1) for i in range(2)]
        firv = cvp
        firg = cvg
        beta_f = sb("beta_f", (4, 512), F32, ph1)
        beta_tm = sb("beta_tm", (128, 16), F32, ph1)
        idg = sb("idg", (4, 512), F32, ph1)
        idsc_bf = sb("idsc_bf", (4, 512), BF16, ph1)
        delta_f2 = [sb(f"delta_f{i}", (128, 8 * 512), BF16, ph1) for i in range(2)]
        fs_f = sb("fs_f", (128, 8 * 512), BF16, ph1)
        fl_f = sb("fl_f", (128, 8 * 512), BF16, ph1)
        o_comb = sb("o_comb", (128, 8 * 512), BF16, ph1)
        S_f32 = sb("S_f32", (128, 2048), F32, ph1)
        S_bf = sb("S_bf", (128, 2048), BF16, ph1)
        dset = []
        for pi in range(2):
            dd = {}
            for nm in ("knt", "vbt", "kbt", "u2", "ws", "wfm", "otm"):
                dd[nm] = sb(f"d_{nm}{pi}", (128, 256), BF16, ph1)
            for nm in ("at", "acu", "acut", "ev1", "evt", "ra", "rb", "att"):
                dd[nm] = sb(f"d_{nm}{pi}", (128, 128), BF16, ph1)
            dset.append(dd)
        sqbf = [sb(f"sqbf{i}", (128, 512), BF16, ph1) for i in range(2)]
        stat_m = sb("stat_m", (12, 512), F32, ph1)
        stat_q = sb("stat_q", (12, 512), F32, ph1)
        stat_v = sb("stat_v", (12, 512), F32, ph1)
        stat_mbf = sb("stat_mbf", (12, 512), BF16, ph1)
        stat_sbf = sb("stat_sbf", (12, 512), BF16, ph1)
        hm_bf = [sb(f"hm{i}", (128, 512), BF16, ph1) for i in range(2)]
        lg = [sb(f"lg{i}", (4, 512), F32, ph1) for i in range(3)]
        ebx = [sb(f"eb{i}", (4, 512), F32, ph1) for i in range(3)]
        pp = [sb(f"pp{i}", (4, 512), BF16, ph1) for i in range(3)]
        mxa = sb("mxa", (4, 512), F32, ph1)
        sma = sb("sma", (4, 512), F32, ph1)
        rms_r = sb("rms_r", (4, 512), F32, ph1)
        rms_rb = sb("rms_rb", (4, 512), BF16, ph1)
        l2acc = sb("l2acc", (4, 512), F32, ph1)
        l2accb = sb("l2accb", (4, 512), BF16, ph1)
        cmb = [sb(f"cmb{i}", (128, 512), BF16, ph1) for i in range(4)]
        oc = [sb(f"oc{i}", (128, 512), BF16, ph1) for i in range(1)]
        wr1st = [sb(f"wr1st{i}", (12, 2048), BF16, ph1) for i in range(2)]
        wr2cs = [sb(f"wr2cs{i}", (128, 64), BF16, ph1) for i in range(3)]
        wsmall = sb("wsmall", (128, 64), BF16, ph1)
        wbuf = [sb(f"wbuf{i}", (128, 8 * 512), BF16, ph1) for i in range(2)]

        p.emit("g", lambda e: e.dma_start(wr1st[0].h[:], wr1m_d.ap()),
               writes=[wr1st[0]], sem="ds1g", dma=True)
        p.emit("g", lambda e: e.dma_start(wr1st[1].h[:], wr1s_d.ap()),
               writes=[wr1st[1]], sem="ds1g", dma=True)
        for cc in range(3):
            p.emit("g", lambda e, cc=cc: e.dma_start(
                wr2cs[cc].h.rearrange("p (k m) -> p k m", k=16),
                wr2c_d[cc].rearrange("(k p) m -> p k m", p=128)),
                writes=[wr2cs[cc]], sem="ds1g", dma=True)
        p.emit("g", lambda e: e.dma_start(
            wsmall.h.rearrange("p (k a) -> p k a", k=8)[:, :, 0:4],
            wb_d.rearrange("(k p) m -> p k m", p=128)),
            writes=[wsmall], sem="ds1g", dma=True)
        p.emit("g", lambda e: e.dma_start(
            wsmall.h.rearrange("p (k a) -> p k a", k=8)[:, :, 4:8],
            wid_d.rearrange("(k p) m -> p k m", p=128)),
            writes=[wsmall], sem="ds1g", dma=True)
        _t1 = Tok("ds1g", p.counts["ds1g"], None)
        for _b in (wr1st[0], wr1st[1], wr2cs[0], wr2cs[1], wr2cs[2], wsmall):
            _b.w = _t1
        S32h = [Buf(S_f32.h, f"S32h{i}") for i in range(4)]
        Sbfh = [Buf(S_bf.h, f"Sbfh{i}") for i in range(4)]
        _tk = p.emit("g", lambda e: e.memset(S_f32.h[:], 0.0), writes=[S_f32],
                     sem="po")
        for _b in S32h:
            _b.w = _tk
        _tk = p.emit("g", lambda e: e.memset(S_bf.h[:], 0.0), writes=[S_bf],
                     sem="po")
        for _b in Sbfh:
            _b.w = _tk
        for nm in ("q", "k", "v"):
            p.emit("g", lambda e, b=zhalo[nm]: e.memset(b.h[:], 0.0),
                   writes=[zhalo[nm]], sem="po")
        p.emit("g", lambda e: e.memset(
            v_f.h.rearrange("p (k c) -> p k c", k=8)[:, :, 0:30], 0.0),
            writes=[v_f], sem="po")

        wload_i = [0]

        def wload(src_ap):
            slot = wload_i[0] % 2
            wb_ = wbuf[slot]
            wload_i[0] += 1
            p.emit("g", lambda e, d=wb_.h, s=src_ap: e.dma_start(
                d.rearrange("p (k m) -> p k m", k=8), s),
                writes=[wb_], sem=f"dwb{slot}", dma=True)
            return wb_

        def fmv(buf, stride, off, ch, t0, n):
            return buf.h[:, ch * stride + off + t0: ch * stride + off + t0 + n]

        def emit_l2norm_tile(buf, stride, off):
            for ch in range(NCH):
                xap = fmv(buf, stride, off, ch, 0, 512)
                p.emit("s", lambda e, a=xap, o=sqbf[ch % 2]: e.activation(
                    o.h[:], a, AF.Square),
                    reads=[buf], writes=[sqbf[ch % 2]], sem="ac")
                bb = pnext()
                mmgrp(bb, bb.h[0:4, 0:512],
                      [(l2red.h[:, ch * 4:(ch + 1) * 4], sqbf[ch % 2].h[:])],
                      [sqbf[ch % 2], l2red])
                if ch == 0:
                    p.emit("v", lambda e, b=bb: e.tensor_copy(
                        l2acc.h[:], b.h[0:4, 0:512]),
                        reads=[bb], writes=[l2acc], sem="dv")
                else:
                    p.emit("v", lambda e, b=bb: e.tensor_tensor(
                        l2acc.h[:], l2acc.h[:], b.h[0:4, 0:512], ALU.add),
                        reads=[bb, l2acc], writes=[l2acc], sem="dv")
            p.emit("s", lambda e: e.activation(
                l2acc.h[:], l2acc.h[:], AF.Sqrt, bias=epsc.h[0:4, 1:2]),
                reads=[l2acc, epsc], writes=[l2acc], sem="ac")
            p.emit("v", [
                lambda e: e.reciprocal(l2acc.h[:], l2acc.h[:]),
                lambda e: e.tensor_copy(l2accb.h[:], l2acc.h[:]),
            ], reads=[l2acc], writes=[l2acc, l2accb], sem="dv")
            for ch in range(NCH):
                h = ch // 2
                bb = pnext()
                mmgrp(bb, bb.h[:, 0:512],
                      [(selh.h[:, h * 128:(h + 1) * 128], l2accb.h[0:4, :])],
                      [selh, l2accb])
                xap = fmv(buf, stride, off, ch, 0, 512)
                p.emit("v", lambda e, a=xap, b=bb: e.tensor_tensor(
                    a, a, b.h[:, 0:512], ALU.mult),
                    reads=[bb, buf], writes=[buf], sem="dv")

        def emit_small_proj(lhs_off, act, bias_ap, dstbuf, xnf):
            bank = pnext()
            items = [(wsmall.h[:, k * 8 + lhs_off: k * 8 + lhs_off + 4],
                      xnf.h[:, k * 512:(k + 1) * 512]) for k in range(NCH)]
            mmgrp(bank, bank.h[0:4, 0:512], items, [wsmall, xnf])
            if bias_ap is None:
                p.emit("s", lambda e: e.activation(
                    dstbuf.h[:], bank.h[0:4, 0:512], act),
                    reads=[bank, smalls], writes=[dstbuf], sem="ac")
            else:
                p.emit("s", lambda e: e.activation(
                    dstbuf.h[:], bank.h[0:4, 0:512], act, bias=bias_ap),
                    reads=[bank, smalls], writes=[dstbuf], sem="ac")

        def gen_proj_conv(wd, dst, dstride, doff, wname, xnf):
            halo = zhalo[wname[-1]]
            for cbi in range(2):
                wb_ = wload(wd.rearrange("(k p) m -> p k m", p=128)
                            [:, :, cbi * 512:(cbi + 1) * 512])
                for mloc in range(4):
                    ch = cbi * 4 + mloc
                    zp = zping[ch % 2]
                    bank = pnext()
                    items = [(wb_.h[:, k * 512 + mloc * 128:
                                    k * 512 + mloc * 128 + 128],
                              xnf.h[:, k * 512:(k + 1) * 512])
                             for k in range(NCH)]
                    mmgrp(bank, bank.h[:, 0:512], items, [wb_, xnf])
                    p.emit("s", lambda e, b=bank, z=zp: e.activation(
                        z.h[:, 3:515], b.h[:, 0:512], AF.Identity),
                        reads=[bank], writes=[zp], sem="ac")
                    p.emit("v", lambda e, z=zp, hl=halo, ch=ch: e.tensor_copy(
                        z.h[:, 0:3], hl.h[:, ch * 3:(ch + 1) * 3]),
                        reads=[halo], writes=[zp], sem="dv")
                    p.emit("v", lambda e, z=zp, hl=halo, ch=ch: e.tensor_copy(
                        hl.h[:, ch * 3:(ch + 1) * 3], z.h[:, 512:515]),
                        reads=[zp], writes=[halo], sem="dv")
                    w0, w1, w2, w3 = (scol(wname, ch * 4 + i, 1) for i in range(4))
                    p.emit("v", [
                        lambda e, z=zp, w=w0: e.tensor_scalar(
                            cvp[0].h[:], z.h[:, 0:512], w, None, ALU.mult),
                        lambda e, z=zp, w=w1: e.scalar_tensor_tensor(
                            cvp[1].h[:], z.h[:, 1:513], w, cvp[0].h[:],
                            ALU.mult, ALU.add),
                        lambda e, z=zp, w=w2: e.scalar_tensor_tensor(
                            cvp[0].h[:], z.h[:, 2:514], w, cvp[1].h[:],
                            ALU.mult, ALU.add),
                        lambda e, z=zp, w=w3: e.scalar_tensor_tensor(
                            cvp[1].h[:], z.h[:, 3:515], w, cvp[0].h[:],
                            ALU.mult, ALU.add),
                    ], reads=[zp, smalls], writes=[cvp[0], cvp[1]], sem="dv")
                    p.emit("s", lambda e, d=fmv(dst, dstride, doff, ch, 0, 512):
                           e.activation(d, cvp[1].h[:], AF.Silu),
                           reads=[cvp[1]], writes=[dst], sem="ac")
                    yield

        def gen_head(t):
            xnf = xn_fm2[t % 2]
            for s_ in range(4):
                xr = x_sub[s_ % 2]
                p.emit("y", lambda e, t=t, s_=s_, d=xr.h: e.dma_start(
                    d.rearrange("p (o dd) -> p o dd", o=1),
                    x_in.rearrange("(u p) d -> p u d", p=128)
                    [:, t * 4 + s_:t * 4 + s_ + 1, :]),
                    writes=[xr], sem=f"dx{s_ % 2}", dma=True)
                ln_norm(xr, 0, xn_tm, (s_ % 2) * 1024, s_)
                tm_to_fm_sub(xn_tm, (s_ % 2) * 1024, xnf, s_ * 128)
                yield
            emit_small_proj(0, AF.Sigmoid, None, beta_f, xnf)
            fns = [(lambda e, c=c: e.transpose(
                PF[5].h[0:128, c * 4:(c + 1) * 4],
                beta_f.h[0:4, c * 128:(c + 1) * 128], ident_f32.h[0:4, 0:4]))
                for c in range(4)]
            p.emit("t", fns, reads=[beta_f, ident_f32], writes=[PF[5]], sem="pe")
            p.emit("v", lambda e: e.tensor_copy(beta_tm.h[:], PF[5].h[:, 0:16]),
                   reads=[PF[5]], writes=[beta_tm], sem="dv")
            yield
            emit_small_proj(4, AF.Sigmoid, scol("bid")[0:4, :], idg, xnf)
            p.emit("v", lambda e: e.tensor_scalar(
                idsc_bf.h[:], idg.h[:], sal.h[0:4, :], 0.06, ALU.mult, ALU.add),
                reads=[idg, sal], writes=[idsc_bf], sem="dv")
            yield
            yield from gen_proj_conv(wq_d, q_f, 512, 0, "conv_q", xnf)
            emit_l2norm_tile(q_f, 512, 0)
            yield
            yield from gen_proj_conv(wk_d, k_f, 512, 0, "conv_k", xnf)
            emit_l2norm_tile(k_f, 512, 0)
            yield
            yield from gen_proj_conv(wv_d, v_f, 542, 30, "conv_v", xnf)

        def gen_fir(t, ch, df):
            base = ch * 542 + 30
            T = 3
            fns = []
            for i in range(T):
                w = scol("fir_s", ch * T + i, 1)
                srcv = v_f.h[:, base - (T - 1) + i: base - (T - 1) + i + 512]
                if i == 0:
                    fns.append(lambda e, s2=srcv, w2=w: e.tensor_scalar(
                        firv[1].h[:], s2, w2, None, ALU.mult))
                elif i < T - 1:
                    fns.append(lambda e, s2=srcv, w2=w: e.scalar_tensor_tensor(
                        firv[0].h[:], s2, w2, firv[1].h[:], ALU.mult, ALU.add))
                else:
                    d2 = fmv(fs_f, 512, 0, ch, 0, 512)
                    fns.append(lambda e, s2=srcv, w2=w, d2=d2:
                               e.scalar_tensor_tensor(d2, s2, w2, firv[0].h[:],
                                                      ALU.mult, ALU.add))
            p.emit("v", fns, reads=[v_f, smalls],
                   writes=[firv[0], firv[1], fs_f], sem="dv")
            yield
            T = 31
            n_dve = 19
            for i in range(n_dve):
                w = scol("fir_l", ch * T + i, 1)
                srcv = v_f.h[:, base - (T - 1) + i: base - (T - 1) + i + 512]
                dstap = firv[(i + 1) % 2]
                if i == 0:
                    p.emit("v", lambda e, s2=srcv, w2=w, d2=dstap:
                           e.tensor_scalar(d2.h[:], s2, w2, None, ALU.mult),
                           reads=[v_f, smalls], writes=[dstap], sem="dv")
                else:
                    prev = firv[i % 2]
                    p.emit("v", lambda e, s2=srcv, w2=w, d2=dstap, pr=prev:
                           e.scalar_tensor_tensor(d2.h[:], s2, w2, pr.h[:],
                                                  ALU.mult, ALU.add),
                           reads=[v_f, smalls, prev], writes=[dstap], sem="dv")
                if i % 4 == 3:
                    yield
            dve_last = firv[n_dve % 2]
            for j, i in enumerate(range(n_dve, T)):
                w = scol("fir_l", ch * T + i, 1)
                srcv = v_f.h[:, base - (T - 1) + i: base - (T - 1) + i + 512]
                prod = cvg[j % 2]
                if j == 0:
                    p.emit("s", lambda e, s2=srcv, w2=w: e.activation(
                        pca[0].h[:], s2, AF.Copy, scale=w2),
                        reads=[v_f, smalls], writes=[pca[0]], sem="ac")
                else:
                    p.emit("s", lambda e, s2=srcv, w2=w, pd=prod: e.activation(
                        pd.h[:], s2, AF.Copy, scale=w2),
                        reads=[v_f, smalls], writes=[prod], sem="ac")
                    pa, pb_ = pca[j % 2], pca[(j + 1) % 2]
                    p.emit("g", lambda e, pd=prod, a=pa, b2=pb_: e.tensor_tensor(
                        a.h[:], b2.h[:], pd.h[:], ALU.add),
                        reads=[prod, pb_], writes=[pa], sem="po")
                if j % 3 == 2:
                    yield
            pool_last = pca[(T - n_dve - 1) % 2]
            d2 = fmv(fl_f, 512, 0, ch, 0, 512)
            p.emit("v", lambda e, a=dve_last, b2=pool_last, d2=d2:
                   e.tensor_tensor(d2, a.h[:], b2.h[:], ALU.add),
                   reads=[dve_last, pool_last], writes=[fl_f], sem="dv")
            yield

        def gen_delta(t, c, h, pipe, df):
            D = dset[pipe]
            PBp = PB if pipe == 0 else PB2
            DPOOL = (0, 1, 2, 3, 4, 5)
            c0 = c * 128
            qv = [fmv(q_f, 512, 0, 2 * h + u, c0, 128) for u in range(2)]
            kv = [fmv(k_f, 512, 0, 2 * h + u, c0, 128) for u in range(2)]
            vv = [fmv(v_f, 542, 30, 2 * h + u, c0, 128) for u in range(2)]
            bcol = beta_tm.h[:, c * 4 + h: c * 4 + h + 1]
            Sb = Sbfh[h]
            S3 = S32h[h]

            b5 = pnext(DPOOL)
            mmgrp(b5, b5.h[:, 0:128],
                  [(kv[u], kv[u]) for u in range(2)], [k_f])
            yield
            p.emit("v", lambda e, b=b5: e.scalar_tensor_tensor(
                D["acu"].h[:], b.h[:, 0:128], bcol, maskLneg.h[:],
                ALU.mult, ALU.mult),
                reads=[b5, beta_tm, maskLneg], writes=[D["acu"]], sem="dv")
            yield
            fns = [(lambda e, i=kv[u], o=PBp.h[:, u * 128:(u + 1) * 128]:
                    e.transpose(o, i, ident_bf.h[:])) for u in range(2)]
            fns += [(lambda e, i=vv[u], o=PBp.h[:, 256 + u * 128:256 + (u + 1) * 128]:
                     e.transpose(o, i, ident_bf.h[:])) for u in range(2)]
            fns.append(lambda e: e.transpose(
                PBp.h[:, 768:896], D["acu"].h[:], ident_bf.h[:]))
            p.emit("t", fns, reads=[k_f, v_f, D["acu"], ident_bf],
                   writes=[PBp], sem="pe")
            yield
            p.emit("v", lambda e: e.tensor_copy(D["knt"].h[:], PBp.h[:, 0:256]),
                   reads=[PBp], writes=[D["knt"]], sem="dv")
            p.emit("v", lambda e: e.tensor_scalar(
                D["vbt"].h[:], PBp.h[:, 256:512], bcol, None, ALU.mult),
                reads=[PBp, beta_tm], writes=[D["vbt"]], sem="dv")
            p.emit("v", lambda e: e.tensor_copy(D["at"].h[:], PBp.h[:, 768:896]),
                reads=[PBp], writes=[D["at"]], sem="dv")
            yield
            p.emit("v", lambda e: e.tensor_scalar(
                D["kbt"].h[:], D["knt"].h[:], bcol, None, ALU.mult),
                reads=[D["knt"], beta_tm], writes=[D["kbt"]], sem="dv")
            p.emit("v", lambda e: e.tensor_tensor(
                D["ra"].h[:], ident_bf.h[:], D["at"].h[:], ALU.add),
                reads=[ident_bf, D["at"]], writes=[D["ra"]], sem="dv")
            yield

            cur, curt = D["acu"], D["at"]
            nslots = [D["acu"], D["ev1"]]
            tslots = [D["acut"], D["evt"]]
            Rcur, Rnxt = D["ra"], D["rb"]
            for lvl in range(1, 7):
                new = nslots[lvl % 2]
                bA = pnext(DPOOL)
                mmgrp(bA, bA.h[:, 0:128], [(curt.h[:], cur.h[:])], [cur, curt])
                ev_eng = "v" if lvl % 2 else "s"
                p.emit(ev_eng, lambda e, b=bA, o=new: (
                    e.tensor_copy(o.h[:], b.h[:, 0:128]) if hasattr(e, "tensor_copy")
                    else e.copy(o.h[:], b.h[:, 0:128])),
                    reads=[bA], writes=[new],
                    sem="dv" if ev_eng == "v" else "ac")
                yield
                if lvl < 6:
                    newt = tslots[lvl % 2]
                    bB = pnext(DPOOL)
                    mmgrp(bB, bB.h[:, 0:128], [(cur.h[:], curt.h[:])], [cur, curt])
                    p.emit("s", lambda e, b=bB, o=newt: e.copy(
                        o.h[:], b.h[:, 0:128]),
                        reads=[bB], writes=[newt], sem="ac")
                    yield
                else:
                    newt = None
                bC = pnext(DPOOL)
                mmgrp(bC, bC.h[:, 0:128], [(new.h[:], Rcur.h[:])], [new, Rcur])
                p.emit("v", lambda e, b=bC, rc=Rcur, rn=Rnxt: e.tensor_tensor(
                    rn.h[:], rc.h[:], b.h[:, 0:128], ALU.add),
                    reads=[bC, Rcur], writes=[Rnxt], sem="dv")
                yield
                Rcur, Rnxt = Rnxt, Rcur
                cur, curt = new, newt

            b6 = pnext(DPOOL)
            fns = [(lambda e, l=D["kbt"].h[:, uu * 128:(uu + 1) * 128],
                    o=b6.h[:, uu * 128:(uu + 1) * 128], r=Rcur.h[:]:
                    e.matmul(o, l, r, start=True, stop=True)) for uu in range(2)]
            p.emit("t", fns, reads=[D["kbt"], Rcur], writes=[b6], sem="pe")
            yield
            p.emit("s", lambda e, b=b6: e.copy(D["wfm"].h[:], b.h[:, 0:256]),
                   reads=[b6], writes=[D["wfm"]], sem="ac")
            yield
            b7 = pnext(DPOOL)
            mmgrp(b7, b7.h[:, 0:256],
                  [(D["wfm"].h[:, kc * 128:(kc + 1) * 128],
                    S_bf.h[:, (h * 2 + kc) * 256:(h * 2 + kc + 1) * 256])
                   for kc in range(2)], [D["wfm"], Sb])
            yield
            p.emit("s", lambda e, b=b7: e.copy(D["ws"].h[:], b.h[:, 0:256]),
                   reads=[b7], writes=[D["ws"]], sem="ac")
            yield
            b8 = pnext(DPOOL)
            mmgrp(b8, b8.h[:, 0:256], [(Rcur.h[:], D["vbt"].h[:])],
                  [Rcur, D["vbt"]])
            yield
            p.emit("v", lambda e, b=b8: e.tensor_tensor(
                D["u2"].h[:], b.h[:, 0:256], D["ws"].h[:], ALU.subtract),
                reads=[b8, D["ws"]], writes=[D["u2"]], sem="dv")
            yield
            b9 = pnext(DPOOL)
            mmgrp(b9, b9.h[:, 0:128],
                  [(kv[u], qv[u]) for u in range(2)], [k_f, q_f])
            yield
            p.emit("v", lambda e, b=b9: e.tensor_tensor(
                D["att"].h[:], maskUincl.h[:], b.h[:, 0:128], ALU.mult),
                reads=[b9, maskUincl], writes=[D["att"]], sem="dv")
            yield
            b10 = pnext(DPOOL)
            items = [(qv[kc],
                      S_bf.h[:, (h * 2 + kc) * 256:(h * 2 + kc + 1) * 256])
                     for kc in range(2)]
            items.append((D["att"].h[:], D["u2"].h[:]))
            mmgrp(b10, b10.h[:, 0:256], items, [q_f, Sb, D["att"], D["u2"]])
            yield
            p.emit("v", lambda e, b=b10: e.tensor_copy(D["otm"].h[:],
                                                       b.h[:, 0:256]),
                   reads=[b10], writes=[D["otm"]], sem="dv")
            yield
            fns = [(lambda e, i=D["otm"].h[:, u * 128:(u + 1) * 128],
                    o=PBp.h[:, 512 + u * 128:512 + (u + 1) * 128]:
                    e.transpose(o, i, ident_bf.h[:])) for u in range(2)]
            p.emit("t", fns, reads=[D["otm"], ident_bf], writes=[PBp], sem="pe")
            yield
            dv = df.h.rearrange("p (k c) -> p k c", k=NCH)[
                :, 2 * h:2 * h + 2, c0:c0 + 128]
            sv = PBp.h.rearrange("p (k c) -> p k c", k=NCH)[:, 4:6, :]
            p.emit("v", lambda e, dv=dv, sv=sv: e.tensor_copy(dv, sv),
                   reads=[PBp], writes=[df], sem="dv")
            yield
            for uu in range(2):
                b11 = pnext(DPOOL)
                mmgrp(b11, b11.h[:, 0:256],
                      [(D["knt"].h[:, uu * 128:(uu + 1) * 128], D["u2"].h[:])],
                      [D["knt"], D["u2"]])
                yield
                ssl = S_f32.h[:, (h * 2 + uu) * 256:(h * 2 + uu + 1) * 256]
                p.emit("v", lambda e, b=b11, s2=ssl: e.tensor_tensor(
                    s2, s2, b.h[:, 0:256], ALU.add),
                    reads=[b11, S3], writes=[S3], sem="dv")
                p.emit("s", lambda e, s2=ssl,
                       d2=S_bf.h[:, (h * 2 + uu) * 256:(h * 2 + uu + 1) * 256]:
                       e.copy(d2, s2),
                       reads=[S3], writes=[Sb], sem="ac")
                yield

        def gen_dfir(t):
            df = delta_f2[t % 2]

            def gen_fir_pair(c):
                yield from gen_fir(t, 2 * c, df)
                yield from gen_fir(t, 2 * c + 1, df)

            def gen_delta_chain(c, pipe):
                yield from gen_delta(t, c, pipe, pipe, df)
                yield from gen_delta(t, c, pipe + 2, pipe, df)

            for c in range(4):
                sub = [gen_fir_pair(c), gen_delta_chain(c, 0),
                       gen_delta_chain(c, 1)]
                while sub:
                    keep = []
                    for g_ in sub:
                        try:
                            next(g_)
                            keep.append(g_)
                        except StopIteration:
                            pass
                    sub = keep
                    yield

        def gen_tail(t):
            xnf = xn_fm2[t % 2]
            df = delta_f2[t % 2]
            ao = df
            for si, X in enumerate((fs_f, fl_f, df)):
                for j in range(NCH):
                    xap = fmv(X, 512, 0, j, 0, 512)
                    p.emit("s", lambda e, a=xap, o=sqbf[j % 2]: e.activation(
                        o.h[:], a, AF.Square),
                        reads=[X], writes=[sqbf[j % 2]], sem="ac")
                    lhs = statred.h[:, (si * 8 + j) * 12:(si * 8 + j + 1) * 12]
                    s0 = (si == 0 and j == 0)
                    s1 = (si == 2 and j == 7)
                    p.emit("t", lambda e, l=lhs, r=xap, s0=s0, s1=s1: e.matmul(
                        PF[0].h[0:12, 0:512], l, r, start=s0, stop=s1),
                        reads=[X, statred], writes=[PF[0]], sem="pe")
                    p.emit("t", lambda e, l=lhs, r=sqbf[j % 2].h[:], s0=s0, s1=s1:
                           e.matmul(PF[1].h[0:12, 0:512], l, r, start=s0, stop=s1),
                           reads=[sqbf[j % 2], statred], writes=[PF[1]], sem="pe")
            p.emit("s", lambda e: e.copy(stat_m.h[:], PF[0].h[0:12, 0:512]),
                   reads=[PF[0]], writes=[stat_m], sem="ac")
            p.emit("s", lambda e: e.copy(stat_q.h[:], PF[1].h[0:12, 0:512]),
                   reads=[PF[1]], writes=[stat_q], sem="ac")
            yield
            p.emit("v", [
                lambda e: e.tensor_tensor(stat_v.h[:], stat_m.h[:], stat_m.h[:],
                                          ALU.mult),
                lambda e: e.tensor_tensor(stat_q.h[:], stat_q.h[:], stat_v.h[:],
                                          ALU.subtract),
            ], reads=[stat_m, stat_q], writes=[stat_v, stat_q], sem="dv")
            p.emit("s", lambda e: e.activation(stat_v.h[:], stat_q.h[:], AF.Sqrt),
                   reads=[stat_q], writes=[stat_v], sem="ac")
            p.emit("v", [
                lambda e: e.tensor_copy(stat_mbf.h[:], stat_m.h[:]),
                lambda e: e.tensor_copy(stat_sbf.h[:], stat_v.h[:]),
            ], reads=[stat_m, stat_v], writes=[stat_mbf, stat_sbf], sem="dv")
            yield
            for cbi in range(4):
                wb_ = wload(wr1x_d.rearrange("(k p) m -> p k m", p=128)
                            [:, :, cbi * 512:(cbi + 1) * 512])
                for mloc in range(4):
                    m = cbi * 4 + mloc
                    bank = pnext(pool=(0, 1, 5))
                    items = [(wb_.h[:, k * 512 + mloc * 128:
                                    k * 512 + mloc * 128 + 128],
                              xnf.h[:, k * 512:(k + 1) * 512])
                             for k in range(NCH)]
                    items.append((wr1st[0].h[:, m * 128:(m + 1) * 128],
                                  stat_mbf.h[:]))
                    items.append((wr1st[1].h[:, m * 128:(m + 1) * 128],
                                  stat_sbf.h[:]))
                    mmgrp(bank, bank.h[:, 0:512], items,
                          [wb_, xnf, wr1st[0], wr1st[1], stat_mbf, stat_sbf])
                    p.emit("s", lambda e, b=bank, m=m, o=hm_bf[m % 2]:
                           e.activation(o.h[:], b.h[:, 0:512], AF.Gelu,
                                        bias=scol("br1", m, 1)),
                           reads=[bank, smalls], writes=[hm_bf[m % 2]], sem="ac")
                    for cc in range(3):
                        p.emit("t", lambda e, cc=cc, m=m, mloc=mloc: e.matmul(
                            PF[2 + cc].h[0:4, 0:512],
                            wr2cs[cc].h[:, m * 4:(m + 1) * 4],
                            hm_bf[m % 2].h[:], start=(mloc == 0),
                            stop=(mloc == 3)),
                            reads=[hm_bf[m % 2], wr2cs[cc]],
                            writes=[PF[2 + cc]], sem="pe")
                for cc in range(3):
                    if cbi == 0:
                        p.emit("v", lambda e, cc=cc: e.tensor_copy(
                            lg[cc].h[:], PF[2 + cc].h[0:4, 0:512]),
                            reads=[PF[2 + cc]], writes=[lg[cc]], sem="dv")
                    else:
                        p.emit("v", lambda e, cc=cc: e.tensor_tensor(
                            lg[cc].h[:], lg[cc].h[:], PF[2 + cc].h[0:4, 0:512],
                            ALU.add),
                            reads=[PF[2 + cc], lg[cc]], writes=[lg[cc]], sem="dv")
                yield
            for cc in range(3):
                p.emit("v", lambda e, cc=cc: e.tensor_scalar(
                    lg[cc].h[:], lg[cc].h[:], scol(f"br2_{cc}")[0:4, 0:1], None,
                    ALU.add),
                    reads=[lg[cc], smalls], writes=[lg[cc]], sem="dv")
            ivc = invt.h[0:4, :]
            p.emit("v", [
                lambda e: e.tensor_scalar(ebx[0].h[:], lg[0].h[:], ivc, None, ALU.mult),
                lambda e: e.tensor_scalar(ebx[1].h[:], lg[1].h[:], ivc, None, ALU.mult),
                lambda e: e.tensor_scalar(ebx[2].h[:], lg[2].h[:], ivc, None, ALU.mult),
                lambda e: e.tensor_tensor(mxa.h[:], ebx[0].h[:], ebx[1].h[:], ALU.max),
                lambda e: e.tensor_tensor(mxa.h[:], mxa.h[:], ebx[2].h[:], ALU.max),
                lambda e: e.tensor_tensor(lg[0].h[:], ebx[0].h[:], mxa.h[:], ALU.subtract),
                lambda e: e.tensor_tensor(lg[1].h[:], ebx[1].h[:], mxa.h[:], ALU.subtract),
                lambda e: e.tensor_tensor(lg[2].h[:], ebx[2].h[:], mxa.h[:], ALU.subtract),
            ], reads=[lg[0], lg[1], lg[2], invt],
               writes=[ebx[0], ebx[1], ebx[2], mxa, lg[0], lg[1], lg[2]],
               sem="dv")
            for cc in range(3):
                p.emit("s", lambda e, cc=cc: e.activation(
                    ebx[cc].h[:], lg[cc].h[:], AF.Exp),
                    reads=[lg[cc]], writes=[ebx[cc]], sem="ac")
            p.emit("v", [
                lambda e: e.tensor_tensor(sma.h[:], ebx[0].h[:], ebx[1].h[:], ALU.add),
                lambda e: e.tensor_tensor(sma.h[:], sma.h[:], ebx[2].h[:], ALU.add),
                lambda e: e.reciprocal(sma.h[:], sma.h[:]),
                lambda e: e.tensor_tensor(lg[0].h[:], ebx[0].h[:], sma.h[:], ALU.mult),
                lambda e: e.tensor_tensor(lg[1].h[:], ebx[1].h[:], sma.h[:], ALU.mult),
                lambda e: e.tensor_tensor(lg[2].h[:], ebx[2].h[:], sma.h[:], ALU.mult),
                lambda e: e.tensor_scalar(pp[0].h[:], lg[0].h[:], 0.925, 0.025,
                                          ALU.mult, ALU.add),
                lambda e: e.tensor_scalar(pp[1].h[:], lg[1].h[:], 0.925, 0.025,
                                          ALU.mult, ALU.add),
                lambda e: e.tensor_scalar(pp[2].h[:], lg[2].h[:], 0.925, 0.025,
                                          ALU.mult, ALU.add),
            ], reads=[ebx[0], ebx[1], ebx[2], lg[0], lg[1], lg[2]],
               writes=[sma, lg[0], lg[1], lg[2], pp[0], pp[1], pp[2]],
               sem="dv")
            yield
            for ch in range(NCH):
                h = ch // 2
                srcs = [(pp[0], fmv(fs_f, 512, 0, ch, 0, 512), fs_f),
                        (pp[1], fmv(fl_f, 512, 0, ch, 0, 512), fl_f),
                        (pp[2], fmv(df, 512, 0, ch, 0, 512), df),
                        (idsc_bf, fmv(v_f, 542, 30, ch, 0, 512), v_f)]
                for i, (pv, xv, xb) in enumerate(srcs):
                    bb = pnext()
                    mmgrp(bb, bb.h[:, 0:512],
                          [(selh.h[:, h * 128:(h + 1) * 128], pv.h[0:4, :])],
                          [selh, pv])
                    p.emit("v", lambda e, b=bb, a=xv, o=cmb[i]: e.tensor_tensor(
                        o.h[:], b.h[:, 0:512], a, ALU.mult),
                        reads=[bb, xb], writes=[cmb[i]], sem="dv")
                p.emit("g", lambda e: e.tensor_tensor(
                    oc[0].h[:], cmb[0].h[:], cmb[1].h[:], ALU.add),
                    reads=[cmb[0], cmb[1]], writes=[oc[0]], sem="po")
                p.emit("g", lambda e: e.tensor_tensor(
                    cmb[0].h[:], cmb[2].h[:], cmb[3].h[:], ALU.add),
                    reads=[cmb[2], cmb[3], cmb[0]], writes=[cmb[0]], sem="po")
                p.emit("v",
                    lambda e, d=fmv(o_comb, 512, 0, ch, 0, 512): e.tensor_tensor(
                        d, oc[0].h[:], cmb[0].h[:], ALU.add),
                    reads=[oc[0], cmb[0]], writes=[o_comb], sem="dv")
                if ch % 2 == 1:
                    yield
            for ch in range(NCH):
                xap = fmv(o_comb, 512, 0, ch, 0, 512)
                p.emit("s", lambda e, a=xap, o=sqbf[ch % 2]: e.activation(
                    o.h[:], a, AF.Square),
                    reads=[o_comb], writes=[sqbf[ch % 2]], sem="ac")
                p.emit("t", lambda e, ch=ch: e.matmul(
                    PF[0].h[0:4, 0:512], rmsred.h[:, ch * 4:(ch + 1) * 4],
                    sqbf[ch % 2].h[:], start=(ch == 0), stop=(ch == 7)),
                    reads=[sqbf[ch % 2], rmsred], writes=[PF[0]], sem="pe")
            p.emit("s", lambda e: e.activation(
                rms_r.h[:], PF[0].h[0:4, 0:512], AF.Sqrt, bias=epsc.h[0:4, 0:1]),
                reads=[PF[0], epsc], writes=[rms_r], sem="ac")
            p.emit("v", [
                lambda e: e.reciprocal(rms_r.h[:], rms_r.h[:]),
                lambda e: e.tensor_copy(rms_rb.h[:], rms_r.h[:]),
            ], reads=[rms_r], writes=[rms_r, rms_rb], sem="dv")
            yield
            for ch in range(NCH):
                h = ch // 2
                bb = pnext()
                mmgrp(bb, bb.h[:, 0:512],
                      [(selh.h[:, h * 128:(h + 1) * 128], rms_rb.h[0:4, :])],
                      [selh, rms_rb])
                ocv = fmv(o_comb, 512, 0, ch, 0, 512)
                onv = xnf.h[:, ch * 512:(ch + 1) * 512]
                wcol = scol("o_norm", ch % 2, 1)
                p.emit("v", [
                    lambda e, b=bb, a=ocv: e.tensor_tensor(
                        cmb[0].h[:], a, b.h[:, 0:512], ALU.mult),
                    lambda e, o2=onv, w=wcol: e.tensor_scalar(
                        o2, cmb[0].h[:], w, None, ALU.mult),
                ], reads=[bb, o_comb, smalls], writes=[cmb[0], xnf], sem="dv")
                if ch % 2 == 1:
                    yield
            for cbi in range(2):
                wb_ = wload(wo_d.rearrange("(k p) m -> p k m", p=128)
                            [:, :, cbi * 512:(cbi + 1) * 512])
                for mloc in range(4):
                    ch = cbi * 4 + mloc
                    bank = pnext()
                    items = [(wb_.h[:, k * 512 + mloc * 128:
                                    k * 512 + mloc * 128 + 128],
                              xnf.h[:, k * 512:(k + 1) * 512])
                             for k in range(NCH)]
                    mmgrp(bank, bank.h[:, 0:512], items, [wb_, xnf])
                    p.emit("s", lambda e, b=bank, d=fmv(ao, 512, 0, ch, 0, 512):
                           e.activation(d, b.h[:, 0:512], AF.Identity),
                           reads=[bank], writes=[ao], sem="ac")
                    yield
            for s_ in range(4):
                xr = x_sub[s_ % 2]
                p.emit("y", lambda e, t=t, s_=s_, d=xr.h: e.dma_start(
                    d.rearrange("p (o dd) -> p o dd", o=1),
                    x_in.rearrange("(u p) d -> p u d", p=128)
                    [:, t * 4 + s_:t * 4 + s_ + 1, :]),
                    writes=[xr], sem=f"dx{s_ % 2}", dma=True)
                fns = []
                for m in range(NCH):
                    inap = ao.h[:, m * 512 + s_ * 128: m * 512 + s_ * 128 + 128]
                    outap = PB.h[:, m * 128:(m + 1) * 128]
                    fns.append(lambda e, i=inap, o=outap:
                               e.transpose(o, i, ident_bf.h[:]))
                p.emit("t", fns, reads=[ao, ident_bf], writes=[PB], sem="pe")
                p.emit("v", lambda e, x0=xr: e.tensor_tensor(
                    x0.h[:], x0.h[:], PB.h[:, 0:1024], ALU.add),
                    reads=[xr, PB], writes=[xr], sem="dv")
                ln_norm(xr, 0, xn_tm, (s_ % 2) * 1024, 4 + s_)
                p.emit("y", lambda e, t=t, s_=s_, x1=xr: e.dma_start(
                    x1s.rearrange("(u p) d -> p u d", p=128)
                    [:, t * 4 + s_:t * 4 + s_ + 1, :],
                    x1.h.rearrange("p (o dd) -> p o dd", o=1)),
                    reads=[xr], sem=f"dxa{s_ % 2}", dma=True)
                tm_to_fm_sub(xn_tm, (s_ % 2) * 1024, xnf, s_ * 128)
                yield
            p.emit("y", lambda e, t=t, xnf=xnf: e.dma_start(
                xn1s.rearrange("(k p) tt -> p k tt", p=128)
                [:, :, t * 512:(t + 1) * 512],
                xnf.h.rearrange("p (k c) -> p k c", k=NCH)),
                reads=[xnf], sem="dxn1", dma=True)
            p.emit("v", lambda e: e.tensor_copy(
                v_f.h.rearrange("p (k c) -> p k c", k=8)[:, :, 0:30],
                v_f.h.rearrange("p (k c) -> p k c", k=8)[:, :, 512:542]),
                reads=[v_f], writes=[v_f], sem="dv")
            yield

        def run_gens(gens, until=None):
            need = set(until if until is not None else gens)
            alive = list(gens)
            while need & set(alive):
                keep = []
                for g_ in alive:
                    try:
                        next(g_)
                        keep.append(g_)
                    except StopIteration:
                        pass
                alive = keep
            return alive

        leftovers = []
        hg = gen_head(0)
        leftovers = run_gens([hg])
        for t in range(NTILE):
            dg = gen_dfir(t)
            leftovers = run_gens([dg] + leftovers, until=[dg])
            newg = []
            if t + 1 < NTILE:
                newg.append(gen_head(t + 1))
            tg = gen_tail(t)
            group = newg + leftovers + [tg]
            leftovers = run_gens(group, until=newg if newg else None)
        run_gens(leftovers)
    else:
        def emit_tile_mini(t):
            for s_ in range(4):
                xr = x_sub[s_ % 2]
                p.emit("y", lambda e, t=t, s_=s_, d=xr.h: e.dma_start(
                    d.rearrange("p (o dd) -> p o dd", o=1),
                    x_in.rearrange("(u p) d -> p u d", p=128)
                    [:, t * 4 + s_:t * 4 + s_ + 1, :]),
                    writes=[xr], sem=f"dx{s_ % 2}", dma=True)
                ln_norm(xr, 0, xn_tm, (s_ % 2) * 1024, s_)
                tm_to_fm_sub(xn_tm, (s_ % 2) * 1024, xn_fm2[0], s_ * 128)
                p.emit("y", lambda e, t=t, s_=s_, x1=xr: e.dma_start(
                    x1s.rearrange("(u p) d -> p u d", p=128)
                    [:, t * 4 + s_:t * 4 + s_ + 1, :],
                    x1.h.rearrange("p (o dd) -> p o dd", o=1)),
                    reads=[xr], sem=f"dxa{s_ % 2}", dma=True)
            p.emit("y", lambda e, t=t: e.dma_start(
                xn1s.rearrange("(k p) tt -> p k tt", p=128)
                [:, :, t * 512:(t + 1) * 512],
                xn_fm2[0].h.rearrange("p (k c) -> p k c", k=NCH)),
                reads=[xn_fm2[0]], sem="dxn1", dma=True)

        for t in range(NTILE):
            emit_tile_mini(t)

    p.barrier()
    ph1.close()

    xn1 = sb("xn1", (128, 8 * 1024), BF16)
    x1tm = sb("x1tm", (128, 8 * 1024), F32)
    hbuf = sb("hbuf", (128, 32 * 1024), BF16)
    oacc = sb("oacc", (128, 8 * 1024), BF16)
    w1b = [sb(f"w1b{i}", (128, 8 * 512), BF16) for i in range(2)]
    w2b = [sb(f"w2b{i}", (128, 32 * 256), BF16) for i in range(2)]
    tmpy = [sb(f"tmpy{i}", (128, 512), BF16) for i in range(2)]
    mb_sb = sb("mb_sb", (128, 1024), mybir.dt.uint8)
    wredb = sb("wredb", (128, 8 * 16), BF16)
    wgf = sb("wgf", (8, 16), F32)
    wgn = sb("wgn", (8, 16), BF16)
    wgsc = sb("wgsc", (8, 4), F32)
    wgnT = sb("wgnT", (16, 8), BF16)
    b1r = sb("b1r_s", (128, 8 * 32), F32)
    b2r = sb("b2r_s", (128, 8 * 8), F32)
    red_bf = sb("red_bf", (16, 1024), BF16)
    lgf = sb("lgf", (8, 1024), F32)
    ltm = sb("ltm", (128, 64), F32)
    mask_tm = sb("mask_tm", (128, 64), BF16)
    extm = sb("extm", (128, 8), F32)
    mask_fm = sb("mask_fm", (8, 1024), BF16)
    scores_tm = sb("scores_tm", (128, 8), F32)
    rt_sc = sb("rt_sc", (128, 2), F32)
    out_tm = [sb(f"out_tm{i}", (128, 1024), F32) for i in range(2)]

    p.emit("g", lambda e: e.dma_start(
        wredb.h.rearrange("p (k m) -> p k m", k=8),
        wred_d.rearrange("(k p) m -> p k m", p=128)),
        writes=[wredb], sem="ds2g", dma=True)
    p.emit("y", lambda e: e.dma_start(wgf.h[:], wg_d.ap()),
           writes=[wgf], sem="ds2", dma=True)
    p.emit("y", lambda e: e.dma_start(
        b1r.h.rearrange("p (e m) -> p e m", e=8), b1_d.rearrange("e p m -> p e m")),
        writes=[b1r], sem="ds2", dma=True)
    p.emit("y", lambda e: e.dma_start(
        b2r.h.rearrange("p (e m) -> p e m", e=8), b2_d.rearrange("e p m -> p e m")),
        writes=[b2r], sem="ds2", dma=True)
    _t2 = Tok("ds2", p.counts["ds2"], None)
    for _b in (wgf, b1r, b2r):
        _b.w = _t2
    wredb.w = Tok("ds2g", p.counts["ds2g"], None)
    wgsq = sb("wgsq", (8, 16), F32)
    p.emit("v", [
        lambda e: e.tensor_tensor(wgsq.h[:], wgf.h[:], wgf.h[:], ALU.mult),
        lambda e: e.reduce_sum(wgsc.h[:, 0:1], wgsq.h[:], axis=AX),
    ], reads=[wgf], writes=[wgsq, wgsc], sem="dv")
    p.emit("s", lambda e: e.activation(wgsc.h[:, 1:2], wgsc.h[:, 0:1], AF.Sqrt),
           reads=[wgsc], writes=[wgsc], sem="ac")
    p.emit("v", [
        lambda e: e.tensor_single_scalar(wgsc.h[:, 1:2], wgsc.h[:, 1:2], 1e-4,
                                         ALU.max),
        lambda e: e.reciprocal(wgsc.h[:, 2:3], wgsc.h[:, 1:2]),
        lambda e: e.tensor_scalar(wgn.h[:], wgf.h[:], wgsc.h[:, 2:3], None,
                                  ALU.mult),
    ], reads=[wgf, wgsc], writes=[wgsc, wgn], sem="dv")
    p.emit("t", lambda e: e.transpose(PB.h[0:16, 0:8], wgn.h[:],
                                      ident_bf.h[0:8, 0:8]),
           reads=[wgn, ident_bf], writes=[PB], sem="pe")
    p.emit("v", lambda e: e.tensor_copy(wgnT.h[:], PB.h[0:16, 0:8]),
           reads=[PB], writes=[wgnT], sem="dv")

    w2load_i = [0]

    def emit_tb(tb):
        p.emit("y", lambda e, tb=tb: e.dma_start(
            x1tm.h.rearrange("p (s d) -> p s d", s=8),
            x1s.rearrange("(u p) d -> p u d", p=128)[:, tb * 8:(tb + 1) * 8, :]),
            writes=[x1tm], sem="dxt", dma=True)
        p.emit("y", lambda e, tb=tb: e.dma_start(
            xn1.h.rearrange("p (k c) -> p k c", k=8),
            xn1s.rearrange("(k p) tt -> p k tt", p=128)
            [:, :, tb * 1024:(tb + 1) * 1024]),
            writes=[xn1], sem="dxn", dma=True)
        p.emit("g", lambda e: e.memset(oacc.h[:], 0.0),
               writes=[oacc], sem="po")

        for half in range(2):
            items = [(wredb.h[:, k * 16:(k + 1) * 16],
                      xn1.h[:, k * 1024 + half * 512: k * 1024 + half * 512 + 512])
                     for k in range(NCH)]
            mmgrp(PF[5], PF[5].h[0:16, 0:512], items, [wredb, xn1])
            p.emit("s", lambda e, half=half: e.activation(
                red_bf.h[:, half * 512:(half + 1) * 512], PF[5].h[0:16, 0:512],
                AF.Identity),
                reads=[PF[5]], writes=[red_bf], sem="ac")
            mmgrp(PF[5], PF[5].h[0:8, 0:512],
                  [(wgnT.h[:], red_bf.h[:, half * 512:(half + 1) * 512])],
                  [wgnT, red_bf])
            p.emit("s", lambda e, half=half: e.activation(
                lgf.h[:, half * 512:(half + 1) * 512], PF[5].h[0:8, 0:512],
                AF.Identity),
                reads=[PF[5]], writes=[lgf], sem="ac")
        fns = [(lambda e, b=b: e.transpose(
            PF[4].h[:, b * 8:(b + 1) * 8], lgf.h[0:8, b * 128:(b + 1) * 128],
            ident_f32.h[0:8, 0:8])) for b in range(8)]
        p.emit("t", fns, reads=[lgf, ident_f32], writes=[PF[4]], sem="pe")
        p.emit("v", lambda e: e.tensor_copy(ltm.h[:], PF[4].h[:, 0:64]),
               reads=[PF[4]], writes=[ltm], sem="dv")
        for b in range(8):
            lv = ltm.h[:, b * 8:(b + 1) * 8]
            mx = rt_sc.h[:, 0:1]
            p.emit("v", [
                lambda e, lv=lv, mx=mx: e.reduce_max(mx, lv, axis=AX),
                lambda e, b=b, lv=lv, mx=mx: e.tensor_scalar(
                    mask_tm.h[:, b * 8:(b + 1) * 8], lv, mx, None, ALU.is_ge),
                lambda e, lv=lv, mx=mx: e.tensor_scalar(
                    extm.h[:], lv, mx, None, ALU.subtract),
            ], reads=[ltm], writes=[rt_sc, mask_tm, extm], sem="dv")
            p.emit("s", lambda e: e.activation(
                extm.h[:], extm.h[:], AF.Exp, accum_out=rt_sc.h[:, 1:2]),
                reads=[extm], writes=[extm, rt_sc], sem="ac")
            p.emit("v", lambda e, b=b: e.reciprocal(
                scores_tm.h[:, b:b + 1], rt_sc.h[:, 1:2]),
                reads=[rt_sc], writes=[scores_tm], sem="dv")
        fns = [(lambda e, b=b: e.transpose(
            PB.h[0:8, b * 128:(b + 1) * 128], mask_tm.h[:, b * 8:(b + 1) * 8],
            ident_bf.h[:])) for b in range(8)]
        p.emit("t", fns, reads=[mask_tm, ident_bf], writes=[PB], sem="pe")
        p.emit("v", lambda e: e.tensor_copy(mask_fm.h[:], PB.h[0:8, 0:1024]),
               reads=[PB], writes=[mask_fm], sem="dv")

        for ex in range(E):
            for half in range(2):
                mmgrp(PF[4], PF[4].h[:, 0:512],
                      [(sele.h[:, ex * 128:(ex + 1) * 128],
                        mask_fm.h[:, half * 512:(half + 1) * 512])],
                      [sele, mask_fm])
                p.emit("v", lambda e, half=half: e.tensor_copy(
                    mb_sb.h[:, half * 512:(half + 1) * 512], PF[4].h[:, 0:512]),
                    reads=[PF[4]], writes=[mb_sb], sem="dv")
            for g in range(8):
                wb_ = w1b[g % 2]
                p.emit("g", lambda e, ex=ex, g=g, d=wb_.h: e.dma_start(
                    d.rearrange("p (k m) -> p k m", k=8),
                    w1_d[ex].rearrange("(k p) m -> p k m", p=128)
                    [:, :, g * 512:(g + 1) * 512]),
                    writes=[wb_], sem=f"dwb{g % 2}", dma=True)
                for mloc in range(4):
                    fch = g * 4 + mloc
                    for half in range(2):
                        bank = PF[(fch * 2 + half) % 4]
                        items = [(wb_.h[:, k * 512 + mloc * 128:
                                        k * 512 + mloc * 128 + 128],
                                  xn1.h[:, k * 1024 + half * 512:
                                        k * 1024 + half * 512 + 512])
                                 for k in range(NCH)]
                        mmgrp(bank, bank.h[:, 0:512], items, [wb_, xn1])
                        p.emit("s", lambda e, b=bank, ex=ex, fch=fch, half=half:
                               e.activation(
                                   hbuf.h[:, fch * 1024 + half * 512:
                                          fch * 1024 + half * 512 + 512],
                                   b.h[:, 0:512], AF.Gelu,
                                   bias=b1r.h[:, ex * 32 + fch:ex * 32 + fch + 1]),
                               reads=[bank, b1r], writes=[hbuf], sem="ac")
            for g2 in range(4):
                wb2 = w2b[g2 % 2]
                for piece in range(2):
                    p.emit("g", lambda e, ex=ex, g2=g2, piece=piece, d=wb2.h:
                           e.dma_start(
                        d.rearrange("p (k m) -> p k m", k=32)
                        [:, piece * 16:(piece + 1) * 16, :],
                        w2_d[ex].rearrange("(k p) m -> p k m", p=128)
                        [:, piece * 16:(piece + 1) * 16,
                         g2 * 256:(g2 + 1) * 256]),
                        writes=[wb2], sem=f"dw2{g2 % 2}", dma=True)
                for m2l in range(2):
                    m2 = g2 * 2 + m2l
                    for half in range(2):
                        bank = PF[4 + (m2l * 2 + half) % 2]
                        items = [(wb2.h[:, kk * 256 + m2l * 128:
                                        kk * 256 + m2l * 128 + 128],
                                  hbuf.h[:, kk * 1024 + half * 512:
                                         kk * 1024 + half * 512 + 512])
                                 for kk in range(32)]
                        mmgrp(bank, bank.h[:, 0:512], items, [wb2, hbuf])
                        ty = tmpy[half]
                        p.emit("s", lambda e, b=bank, ex=ex, m2=m2, ty=ty:
                               e.activation(ty.h[:], b.h[:, 0:512], AF.Identity,
                                            bias=b2r.h[:, ex * 8 + m2:
                                                       ex * 8 + m2 + 1]),
                               reads=[bank, b2r], writes=[ty], sem="ac")
                        ov = oacc.h[:, m2 * 1024 + half * 512:
                                    m2 * 1024 + half * 512 + 512]
                        mv = mb_sb.h[:, half * 512:(half + 1) * 512]
                        p.emit("v", lambda e, ty=ty, ov=ov, mv=mv:
                               e.copy_predicated(ov, mv, ty.h[:]),
                               reads=[ty, mb_sb, oacc], writes=[oacc], sem="dv")
        for s_ in range(8):
            fns = [(lambda e, m2=m2, s_=s_: e.transpose(
                PB.h[:, m2 * 128:(m2 + 1) * 128],
                oacc.h[:, m2 * 1024 + s_ * 128: m2 * 1024 + s_ * 128 + 128],
                ident_bf.h[:])) for m2 in range(8)]
            p.emit("t", fns, reads=[oacc, ident_bf], writes=[PB], sem="pe")
            ot = out_tm[s_ % 2]
            p.emit("v", lambda e, s_=s_, ot=ot: e.scalar_tensor_tensor(
                ot.h[:], PB.h[:, 0:1024], scores_tm.h[:, s_:s_ + 1],
                x1tm.h[:, s_ * 1024:(s_ + 1) * 1024], ALU.mult, ALU.add),
                reads=[PB, scores_tm, x1tm], writes=[ot], sem="dv")
            p.emit("y", lambda e, tb=tb, s_=s_, ot=ot: e.dma_start(
                out_d.rearrange("(u p) d -> p u d", p=128)
                [:, tb * 8 + s_: tb * 8 + s_ + 1, :],
                ot.h.rearrange("p (o dd) -> p o dd", o=1)),
                reads=[ot], sem=f"do{s_ % 2}", dma=True)

    for tb in range(NTB):
        emit_tb(tb)

    p.barrier()

    with contextlib.ExitStack() as semstack:
        for s in sorted(p.counts.keys()):
            p.sems[s] = semstack.enter_context(nc.semaphore(s))
        with nc.Block() as block:
            @block.tensor
            def _(eng):
                for f in p.ops["t"]:
                    f(eng)

            @block.vector
            def _(eng):
                for f in p.ops["v"]:
                    f(eng)

            @block.scalar
            def _(eng):
                for f in p.ops["s"]:
                    f(eng)

            @block.gpsimd
            def _(eng):
                for f in p.ops["g"]:
                    f(eng)

            @block.sync
            def _(eng):
                for f in p.ops["y"]:
                    f(eng)

    st.close()
    return nc


_CACHE = {}


def kernel(**inputs):
    inp = {k: np.asarray(v) for k, v in inputs.items()}
    smalls, smeta = build_smalls(inp)
    key = "nc"
    if key not in _CACHE:
        consts = build_consts()
        _CACHE[key] = build_nc(consts, smeta, smalls.shape)
    nc = _CACHE[key]

    f32 = np.float32

    def c_(a):
        return np.ascontiguousarray(np.asarray(a, f32))

    wr1 = c_(inp["L0_attn_Wr1"])
    mrows = np.r_[1024:1028, 1032:1036, 1040:1044]
    srows = np.r_[1028:1032, 1036:1040, 1044:1048]
    wr2 = c_(inp["L0_attn_Wr2"])
    base = {
        "wq": c_(inp["L0_attn_Wq"]), "wk": c_(inp["L0_attn_Wk"]),
        "wv": c_(inp["L0_attn_Wv"]), "wb": c_(inp["L0_attn_Wb"]),
        "wid": c_(inp["L0_attn_Wid"]), "wo": c_(inp["L0_attn_Wo"]),
        "wr1x": c_(wr1[0:1024]), "wr1m": c_(wr1[mrows]), "wr1s": c_(wr1[srows]),
        "wr2c0": c_(wr2[:, 0::3]), "wr2c1": c_(wr2[:, 1::3]),
        "wr2c2": c_(wr2[:, 2::3]),
        "wred": c_(inp["L0_ffn_w_red"]), "wg": c_(inp["L0_ffn_wg"]),
        "w1": c_(inp["L0_ffn_w1"]), "w2": c_(inp["L0_ffn_w2"]),
        "b1r": c_(np.asarray(inp["L0_ffn_b1"], f32).reshape(E, 32, 128)
                  .transpose(0, 2, 1)),
        "b2r": c_(np.asarray(inp["L0_ffn_b2"], f32).reshape(E, 8, 128)
                  .transpose(0, 2, 1)),
        "smalls": smalls,
    }
    x = np.asarray(inp["L0_attn_hidden_states"], f32)
    in_maps = []
    for i in range(8):
        m = dict(base)
        m["x"] = np.ascontiguousarray(x[i])
        in_maps.append(m)
    res = run_bass_kernel_spmd(nc, in_maps, core_ids=list(range(8)))
    out = np.stack([r["out"] for r in res.results], axis=0)
    return out


# version 06
# speedup vs baseline: 1.0804x; 1.0804x over previous
import contextlib
import numpy as np
import ml_dtypes
import concourse.bass as bass
import concourse.mybir as mybir
from concourse.bass_utils import run_bass_kernel_spmd

F32 = mybir.dt.float32
BF16 = mybir.dt.bfloat16
AF = mybir.ActivationFunctionType
ALU = mybir.AluOpType
AX = mybir.AxisListType.X

L, D = 4096, 1024
NCH = 8
TILE = 512
NTILE = L // TILE
E, FF = 8, 4096
TB = 1024
NTB = L // TB

import os
FULL_ATTN = os.environ.get("KBASS_MINI", "") != "1"
DEBUG_DUMP = os.environ.get("KBASS_DEBUG", "") == "1"


class Tok:
    __slots__ = ("sem", "val", "eng")

    def __init__(self, sem, val, eng):
        self.sem, self.val, self.eng = sem, val, eng


class Buf:
    __slots__ = ("h", "w", "readers", "name")

    def __init__(self, h, name=""):
        self.h, self.name = h, name
        self.w = None
        self.readers = {}


class Prog:
    ENGS = ("t", "v", "s", "g", "y")
    ESEM = {"t": "pe", "v": "dv", "s": "ac", "g": "po", "y": "sy"}

    def __init__(self):
        self.ops = {e: [] for e in self.ENGS}
        self.counts = {}
        self.hw = {e: {} for e in self.ENGS}
        self.sems = {}

    def _wait(self, eng, tok):
        if tok is None:
            return
        if tok.eng == eng and eng == "t":
            return
        if self.hw[eng].get(tok.sem, 0) >= tok.val:
            return
        self.hw[eng][tok.sem] = tok.val
        s, v = tok.sem, tok.val
        self.ops[eng].append(lambda e: e.wait_ge(self.sems[s], v))

    def emit(self, eng, fns, reads=(), writes=(), sem=None, dma=False,
             extra_waits=()):
        if callable(fns):
            fns = [fns]
        for tok in extra_waits:
            self._wait(eng, tok)
        for b in reads:
            self._wait(eng, b.w)
        for b in writes:
            self._wait(eng, b.w)
            for tok in list(b.readers.values()):
                self._wait(eng, tok)
        tok = None
        amt = 16 if dma else 1
        n = len(fns)
        if sem is not None and not dma and eng != "t" and n > 1:
            for i, f in enumerate(fns):
                self.counts[sem] = self.counts.get(sem, 0) + 1
                v = self.counts[sem]
                self.ops[eng].append(
                    lambda e, f=f, s=sem: f(e).then_inc(self.sems[s], 1))
                if i < n - 1:
                    self.ops[eng].append(
                        lambda e, s=sem, v=v: e.wait_ge(self.sems[s], v))
            tok = Tok(sem, self.counts[sem], eng)
            self.hw[eng][sem] = self.counts[sem] - 1
        else:
            if sem is not None:
                self.counts[sem] = self.counts.get(sem, 0) + amt
                tok = Tok(sem, self.counts[sem], None if dma else eng)
            for i, f in enumerate(fns):
                if sem is not None and i == n - 1:
                    self.ops[eng].append(
                        lambda e, f=f, s=sem, a=amt: f(e).then_inc(self.sems[s], a))
                else:
                    self.ops[eng].append(f)
        if tok is not None:
            for b in reads:
                old = b.readers.get(tok.sem)
                if old is None or old.val < tok.val:
                    b.readers[tok.sem] = tok
            for b in writes:
                b.w = tok
                b.readers = {}
        return tok

    def barrier(self):
        for eng in self.ENGS:
            for sem in sorted(self.counts.keys()):
                tot = self.counts.get(sem, 0)
                if tot > self.hw[eng].get(sem, 0):
                    self.hw[eng][sem] = tot
                    self.ops[eng].append(
                        lambda e, s=sem, v=tot: e.wait_ge(self.sems[s], v))


def _bf(x):
    return np.ascontiguousarray(np.asarray(x, np.float32).astype(ml_dtypes.bfloat16))


def build_consts():
    c = {}
    c["ident_bf"] = _bf(np.eye(128))
    c["ident_f32"] = np.ascontiguousarray(np.eye(128, dtype=np.float32))
    c["ones_1x128"] = _bf(np.ones((1, 128)))
    c["ones_128x1"] = _bf(np.ones((128, 1)))
    selh = np.zeros((4, 512), np.float32)
    for h in range(4):
        selh[h, h * 128:(h + 1) * 128] = 1.0
    c["selh"] = _bf(selh)
    sele = np.zeros((8, 1024), np.float32)
    for e in range(8):
        sele[e, e * 128:(e + 1) * 128] = 1.0
    c["sele"] = _bf(sele)
    mU = np.triu(np.ones((128, 128), np.float32), 1)
    c["maskUneg"] = _bf(-mU)
    c["maskLneg"] = _bf(-mU.T)
    c["maskUincl"] = _bf(np.triu(np.ones((128, 128), np.float32), 0))
    sred = np.zeros((3, 8, 128, 12), np.float32)
    for s in range(3):
        for j in range(8):
            sred[s, j, :, s * 4 + j // 2] = 1.0 / 256.0
    c["statred"] = _bf(sred.reshape(24, 128, 12))
    rred = np.zeros((8, 128, 4), np.float32)
    for j in range(8):
        rred[j, :, j // 2] = 1.0 / 256.0
    c["rmsred"] = _bf(rred)
    l2r_ = np.zeros((8, 128, 4), np.float32)
    for j in range(8):
        l2r_[j, :, j // 2] = 1.0
    c["l2red"] = _bf(l2r_)
    return c


def build_smalls(inp):
    cols = []
    meta = {}

    def add(name, arr):
        arr = np.ascontiguousarray(np.asarray(arr, np.float32))
        assert arr.shape[0] == 128, (name, arr.shape)
        meta[name] = (sum(a.shape[1] for a in cols), arr.shape[1])
        cols.append(arr)

    for nm in ("conv_q", "conv_k", "conv_v"):
        w = np.asarray(inp[f"L0_attn_{nm}"])
        add(nm, w.reshape(8, 128, 4).transpose(1, 0, 2).reshape(128, 32))
    fsf = np.asarray(inp["L0_attn_fir_short"]).reshape(1024, 3)
    add("fir_s", fsf.reshape(8, 128, 3).transpose(1, 0, 2).reshape(128, 24))
    flf = np.asarray(inp["L0_attn_fir_long"]).reshape(1024, 31)
    add("fir_l", flf.reshape(8, 128, 31).transpose(1, 0, 2).reshape(128, 248))
    add("o_norm", np.asarray(inp["L0_attn_o_norm_w"]).reshape(2, 128).T)
    add("br1", np.asarray(inp["L0_attn_br1"]).reshape(16, 128).T)

    def pad4(v):
        z = np.zeros((128, 1), np.float32)
        z[:4, 0] = np.asarray(v, np.float32)
        return z

    add("alpha_id", pad4(inp["L0_attn_alpha_id"]))
    add("bid", pad4(inp["L0_attn_bid"]))
    add("log_tau", pad4(np.asarray(inp["L0_attn_log_tau_group"])[[0, 0, 1, 1]]))
    br2 = np.asarray(inp["L0_attn_br2"])
    for cc in range(3):
        add(f"br2_{cc}", pad4(br2[cc::3]))
    return np.ascontiguousarray(np.concatenate(cols, 1)), meta


def build_nc(consts, smeta, smalls_shape):
    nc = bass.Bass("TRN2", target_bir_lowering=False)
    st = contextlib.ExitStack()
    p = Prog()

    def din(name, shape, dtype=F32):
        return nc.dram_tensor(name, list(shape), dtype, kind="ExternalInput")

    x_in = din("x", (L, D))
    wq_d = din("wq", (D, D)); wk_d = din("wk", (D, D)); wv_d = din("wv", (D, D))
    wb_d = din("wb", (D, 4)); wid_d = din("wid", (D, 4)); wo_d = din("wo", (D, D))
    wr1x_d = din("wr1x", (D, 2048))
    wr1m_d = din("wr1m", (12, 2048)); wr1s_d = din("wr1s", (12, 2048))
    wr2c_d = [din(f"wr2c{i}", (2048, 4)) for i in range(3)]
    wred_d = din("wred", (D, 16)); wg_d = din("wg", (8, 16))
    w1_d = din("w1", (E, D, FF)); w2_d = din("w2", (E, FF, D))
    b1_d = din("b1r", (E, 128, 32)); b2_d = din("b2r", (E, 128, 8))
    smalls_d = din("smalls", smalls_shape)

    out_d = nc.dram_tensor("out", [L, D], F32, kind="ExternalOutput")
    x1s = nc.dram_tensor("x1s", [L, D], F32, kind="Internal")
    xn1s = nc.dram_tensor("xn1s", [D, L], BF16, kind="Internal")
    dbg = {}
    if DEBUG_DUMP:
        for nm, shp, dt_ in (("q", (D, L), BF16), ("k", (D, L), BF16),
                             ("v", (D, L), BF16), ("fs", (D, L), BF16),
                             ("fl", (D, L), BF16), ("delta", (D, L), BF16),
                             ("beta", (4, L), F32), ("idsc", (4, L), BF16),
                             ("statm", (12, L), F32), ("stats", (12, L), F32),
                             ("pp", (12, L), BF16), ("ocomb", (D, L), BF16),
                             ("on", (D, L), BF16), ("xn", (D, L), BF16)):
            dbg[nm] = nc.dram_tensor("dbg_" + nm, list(shp), dt_, kind="Internal")
        for nm, shp, dt_ in (("d_qn", (128, 256), BF16), ("d_kn", (128, 256), BF16),
                             ("d_kb", (128, 512), BF16), ("d_r", (128, 128), BF16),
                             ("d_u2", (128, 512), BF16), ("d_wfm", (128, 512), BF16),
                             ("d_att", (128, 128), BF16), ("d_otm", (128, 256), BF16),
                             ("d_at", (128, 128), BF16), ("d_S", (128, 512), BF16),
                             ("d_vbt", (128, 256), BF16), ("d_knt", (128, 256), BF16)):
            dbg[nm] = nc.dram_tensor("dbg_" + nm, list(shp), dt_, kind="Internal")

    cT = {k: nc.inline_tensor(v, f"c_{k}") for k, v in consts.items()}

    def sb(name, shape, dtype, stack=st):
        return Buf(stack.enter_context(
            nc.sbuf_tensor("sb_" + name, list(shape), dtype)), name)

    def psum(name, shape, dtype):
        return Buf(st.enter_context(nc.psum_tensor(name, list(shape), dtype)),
                   name)

    ident_bf = sb("ident_bf", (128, 128), BF16)
    ident_f32 = sb("ident_f32", (128, 128), F32)
    ones1x = sb("ones1x", (1, 128), BF16)
    ones_c = sb("ones_c", (128, 1), BF16)
    selh = sb("selh", (4, 512), BF16)
    sele = sb("sele", (8, 1024), BF16)
    maskUneg = sb("maskUneg", (128, 128), BF16)
    maskLneg = sb("maskLneg", (128, 128), BF16)
    maskUincl = sb("maskUincl", (128, 128), BF16)
    statred = sb("statred", (128, 24 * 12), BF16)
    rmsred = sb("rmsred", (128, 32), BF16)
    l2red = sb("l2red", (128, 32), BF16)
    smalls = sb("smalls", smalls_shape, F32)
    sal = sb("sal", (4, 1), F32)
    invt = sb("invt", (4, 1), F32)
    epsc = sb("epsc", (128, 2), F32)

    PF = [psum(f"pf{i}", (128, 512), F32) for i in range(6)]
    PB = psum("pb", (128, 1024), BF16)
    PB2 = psum("pb2", (128, 1024), BF16)

    def scol(name, j0=0, n=None):
        base, width = smeta[name]
        if n is None:
            n = width
        return smalls.h[:, base + j0: base + j0 + n]

    _setup_bufs = []
    for bufb, key in ((ident_bf, "ident_bf"), (ident_f32, "ident_f32"),
                      (ones1x, "ones_1x128"), (ones_c, "ones_128x1"),
                      (selh, "selh"), (sele, "sele"), (maskUneg, "maskUneg"),
                      (maskLneg, "maskLneg"), (maskUincl, "maskUincl")):
        p.emit("y", lambda e, d=bufb.h, s=cT[key]: e.dma_start(d[:], s.ap()),
               writes=[bufb], sem="ds1", dma=True)
        _setup_bufs.append(bufb)
    p.emit("y", lambda e: e.dma_start(
        statred.h.rearrange("p (a m) -> p a m", a=24),
        cT["statred"].rearrange("a p m -> p a m")),
        writes=[statred], sem="ds1", dma=True)
    p.emit("y", lambda e: e.dma_start(
        rmsred.h.rearrange("p (a m) -> p a m", a=8),
        cT["rmsred"].rearrange("a p m -> p a m")),
        writes=[rmsred], sem="ds1", dma=True)
    p.emit("y", lambda e: e.dma_start(
        l2red.h.rearrange("p (a m) -> p a m", a=8),
        cT["l2red"].rearrange("a p m -> p a m")),
        writes=[l2red], sem="ds1", dma=True)
    _setup_bufs.append(l2red)
    p.emit("y", lambda e: e.dma_start(smalls.h[:], smalls_d.ap()),
           writes=[smalls], sem="ds1", dma=True)
    _setup_bufs += [statred, rmsred, smalls]
    _t0 = Tok("ds1", p.counts["ds1"], None)
    for _b in _setup_bufs:
        _b.w = _t0
    _setup_bufs = []
    p.emit("v", [
        lambda e: e.memset(epsc.h[:, 0:1], 1e-5),
        lambda e: e.memset(epsc.h[:, 1:2], 1e-6),
    ], writes=[epsc], sem="dv")
    p.emit("s", lambda e: e.activation(sal.h[:], scol("alpha_id")[0:4, :],
                                       AF.Sigmoid),
           reads=[smalls], writes=[sal], sem="ac")
    p.emit("s", lambda e: e.activation(invt.h[:], scol("log_tau")[0:4, :],
                                       AF.Exp, scale=-1.0),
           reads=[smalls], writes=[invt], sem="ac")

    rot = [0]

    def pnext(pool=(2, 3, 4, 5)):
        b = PF[pool[rot[0] % len(pool)]]
        rot[0] += 1
        return b

    def mmgrp(bank, out_ap, items, reads, eng_extra=()):
        n = len(items)
        fns = [(lambda e, l=l, r=r, s0=(i == 0), s1=(i == n - 1):
                e.matmul(out_ap, l, r, start=s0, stop=s1))
               for i, (l, r) in enumerate(items)]
        return p.emit("t", fns, reads=list(reads), writes=[bank], sem="pe",
                      extra_waits=eng_extra)

    ph1 = contextlib.ExitStack()
    x_sub = [sb("x_sub0", (128, 1024), F32, ph1)]
    xn_tm = sb("xn_tm", (128, 2048), BF16, ph1)
    xn_fm2 = [sb(f"xn_fm{i}", (128, 8 * 512), BF16, ph1) for i in range(2)]
    ln_sc = sb("ln_sc", (128, 32), F32, ph1)
    sqln = sb("sqln", (128, 1024), BF16, ph1)

    def ln_norm(src, scol0, dst, dcol0, sidx):
        xap = src.h[:, scol0:scol0 + 1024]
        nap = dst.h[:, dcol0:dcol0 + 1024]
        su = ln_sc.h[:, sidx * 4: sidx * 4 + 1]
        sq = ln_sc.h[:, sidx * 4 + 1: sidx * 4 + 2]
        mm = ln_sc.h[:, sidx * 4 + 2: sidx * 4 + 3]
        rr = ln_sc.h[:, sidx * 4 + 3: sidx * 4 + 4]
        p.emit("v", lambda e: e.reduce_sum(su, xap, axis=AX),
               reads=[src], writes=[ln_sc], sem="dv")
        p.emit("s", lambda e: e.activation(sqln.h[:], xap, AF.Square,
                                           accum_out=sq),
               reads=[src], writes=[ln_sc, sqln], sem="ac")
        p.emit("v", [
            lambda e: e.tensor_single_scalar(mm, su, 1.0 / 1024, ALU.mult),
            lambda e: e.tensor_single_scalar(sq, sq, 1.0 / 1024, ALU.mult),
            lambda e: e.tensor_tensor(rr, mm, mm, ALU.mult),
            lambda e: e.tensor_tensor(sq, sq, rr, ALU.subtract),
        ], reads=[ln_sc], writes=[ln_sc], sem="dv")
        p.emit("s", lambda e, sq=sq: e.activation(sq, sq, AF.Sqrt,
               bias=epsc.h[:, 0:1]),
               reads=[ln_sc, epsc], writes=[ln_sc], sem="ac")
        p.emit("v", [
            lambda e: e.reciprocal(rr, sq),
            lambda e: e.tensor_scalar(nap, xap, mm, rr, ALU.subtract, ALU.mult),
        ], reads=[src, ln_sc], writes=[dst, ln_sc], sem="dv")

    def tm_to_fm_sub(src, scol0, dst, tok_in_tile, n_sub_tok=128):
        fns = []
        for k in range(NCH):
            inap = src.h[:, scol0 + k * 128: scol0 + (k + 1) * 128]
            outap = PB.h[:, k * 128:(k + 1) * 128]
            fns.append(lambda e, i=inap, o=outap: e.transpose(o, i, ident_bf.h[:]))
        p.emit("t", fns, reads=[src, ident_bf], writes=[PB], sem="pe")
        dv = dst.h.rearrange("p (k c) -> p k c", k=NCH)[
            :, :, tok_in_tile:tok_in_tile + 128]
        sv = PB.h.rearrange("p (k c) -> p k c", k=NCH)
        p.emit("v", lambda e: e.tensor_copy(dv, sv),
               reads=[PB], writes=[dst], sem="dv")

    if FULL_ATTN:
        zping = [sb(f"zping{i}", (128, 515), BF16, ph1) for i in range(2)]
        zhalo = {nm: sb(f"zhalo_{nm}", (128, 24), BF16, ph1)
                 for nm in ("q", "k", "v")}
        q_f = sb("q_f", (128, 8 * 512), BF16, ph1)
        k_f = sb("k_f", (128, 8 * 512), BF16, ph1)
        v_f2 = [sb(f"v_f{i}", (128, 8 * 542), BF16, ph1) for i in range(2)]
        cvp = [sb(f"cvp{i}", (128, 512), BF16, ph1) for i in range(2)]
        firv = cvp
        beta_f = sb("beta_f", (4, 512), F32, ph1)
        beta_tm = sb("beta_tm", (128, 16), F32, ph1)
        idg = sb("idg", (4, 512), F32, ph1)
        idsc_bf = sb("idsc_bf", (4, 512), BF16, ph1)
        delta_f2 = [sb(f"delta_f{i}", (128, 8 * 512), BF16, ph1) for i in range(2)]
        fs_f = sb("fs_f", (128, 8 * 512), BF16, ph1)
        fl_f = sb("fl_f", (128, 8 * 512), BF16, ph1)
        o_comb = sb("o_comb", (128, 8 * 512), BF16, ph1)
        S_f32 = sb("S_f32", (128, 2048), F32, ph1)
        S_bf = sb("S_bf", (128, 2048), BF16, ph1)
        dset = []
        for pi in range(2):
            dd = {}
            for nm in ("knt", "vbt", "kbt", "u2", "ws", "wfm", "otm"):
                dd[nm] = sb(f"d_{nm}{pi}", (128, 256), BF16, ph1)
            for nm in ("at", "acu", "acut", "ev1", "evt", "ra", "rb", "att"):
                dd[nm] = sb(f"d_{nm}{pi}", (128, 128), BF16, ph1)
            dset.append(dd)
        sqbf = [sb(f"sqbf{i}", (128, 512), BF16, ph1) for i in range(2)]
        stat_m = sb("stat_m", (12, 512), F32, ph1)
        stat_q = sb("stat_q", (12, 512), F32, ph1)
        stat_v = sb("stat_v", (12, 512), F32, ph1)
        stat_mbf = sb("stat_mbf", (12, 512), BF16, ph1)
        stat_sbf = sb("stat_sbf", (12, 512), BF16, ph1)
        hm_bf = [sb("hm0", (128, 512), BF16, ph1)]
        lg = [sb(f"lg{i}", (4, 512), F32, ph1) for i in range(3)]
        ebx = [sb(f"eb{i}", (4, 512), F32, ph1) for i in range(3)]
        pp = [sb(f"pp{i}", (4, 512), BF16, ph1) for i in range(3)]
        mxa = sb("mxa", (4, 512), F32, ph1)
        sma = sb("sma", (4, 512), F32, ph1)
        rms_r = sb("rms_r", (4, 512), F32, ph1)
        rms_rb = sb("rms_rb", (4, 512), BF16, ph1)
        l2acc = sb("l2acc", (4, 512), F32, ph1)
        l2accb = sb("l2accb", (4, 512), BF16, ph1)
        cmb = [sb(f"cmb{i}", (128, 512), BF16, ph1) for i in range(4)]
        oc = [sb(f"oc{i}", (128, 512), BF16, ph1) for i in range(1)]
        wr1st = [sb(f"wr1st{i}", (12, 2048), BF16, ph1) for i in range(2)]
        wr2cs = [sb(f"wr2cs{i}", (128, 64), BF16, ph1) for i in range(3)]
        wsmall = sb("wsmall", (128, 64), BF16, ph1)
        wbuf = [sb(f"wbuf{i}", (128, 8 * 512), BF16, ph1) for i in range(2)]
        if os.environ.get("KBASS_MEM"):
            print("phase1 sbuf remaining:", nc.sbuf_bytes_remaining)

        p.emit("g", lambda e: e.dma_start(wr1st[0].h[:], wr1m_d.ap()),
               writes=[wr1st[0]], sem="ds1g", dma=True)
        p.emit("g", lambda e: e.dma_start(wr1st[1].h[:], wr1s_d.ap()),
               writes=[wr1st[1]], sem="ds1g", dma=True)
        for cc in range(3):
            p.emit("g", lambda e, cc=cc: e.dma_start(
                wr2cs[cc].h.rearrange("p (k m) -> p k m", k=16),
                wr2c_d[cc].rearrange("(k p) m -> p k m", p=128)),
                writes=[wr2cs[cc]], sem="ds1g", dma=True)
        p.emit("g", lambda e: e.dma_start(
            wsmall.h.rearrange("p (k a) -> p k a", k=8)[:, :, 0:4],
            wb_d.rearrange("(k p) m -> p k m", p=128)),
            writes=[wsmall], sem="ds1g", dma=True)
        p.emit("g", lambda e: e.dma_start(
            wsmall.h.rearrange("p (k a) -> p k a", k=8)[:, :, 4:8],
            wid_d.rearrange("(k p) m -> p k m", p=128)),
            writes=[wsmall], sem="ds1g", dma=True)
        _t1 = Tok("ds1g", p.counts["ds1g"], None)
        for _b in (wr1st[0], wr1st[1], wr2cs[0], wr2cs[1], wr2cs[2], wsmall):
            _b.w = _t1
        S32h = [Buf(S_f32.h, f"S32h{i}") for i in range(4)]
        Sbfh = [Buf(S_bf.h, f"Sbfh{i}") for i in range(4)]
        _tk = p.emit("g", lambda e: e.memset(S_f32.h[:], 0.0), writes=[S_f32],
                     sem="po")
        for _b in S32h:
            _b.w = _tk
        _tk = p.emit("g", lambda e: e.memset(S_bf.h[:], 0.0), writes=[S_bf],
                     sem="po")
        for _b in Sbfh:
            _b.w = _tk
        for nm in ("q", "k", "v"):
            p.emit("g", lambda e, b=zhalo[nm]: e.memset(b.h[:], 0.0),
                   writes=[zhalo[nm]], sem="po")
        for vv_ in v_f2:
            p.emit("g", lambda e, vv_=vv_: e.memset(
                vv_.h.rearrange("p (k c) -> p k c", k=8)[:, :, 0:30], 0.0),
                writes=[vv_], sem="po")

        wload_i = [0]

        def wload(src_ap):
            slot = wload_i[0] % 2
            wb_ = wbuf[slot]
            wload_i[0] += 1
            p.emit("g", lambda e, d=wb_.h, s=src_ap: e.dma_start(
                d.rearrange("p (k m) -> p k m", k=8), s),
                writes=[wb_], sem=f"dwb{slot}", dma=True)
            return wb_

        def fmv(buf, stride, off, ch, t0, n):
            return buf.h[:, ch * stride + off + t0: ch * stride + off + t0 + n]

        def emit_l2norm_tile(buf, stride, off):
            for ch in range(NCH):
                xap = fmv(buf, stride, off, ch, 0, 512)
                p.emit("s", lambda e, a=xap, o=sqbf[ch % 2]: e.activation(
                    o.h[:], a, AF.Square),
                    reads=[buf], writes=[sqbf[ch % 2]], sem="ac")
                bb = pnext()
                mmgrp(bb, bb.h[0:4, 0:512],
                      [(l2red.h[:, ch * 4:(ch + 1) * 4], sqbf[ch % 2].h[:])],
                      [sqbf[ch % 2], l2red])
                if ch == 0:
                    p.emit("v", lambda e, b=bb: e.tensor_copy(
                        l2acc.h[:], b.h[0:4, 0:512]),
                        reads=[bb], writes=[l2acc], sem="dv")
                else:
                    p.emit("v", lambda e, b=bb: e.tensor_tensor(
                        l2acc.h[:], l2acc.h[:], b.h[0:4, 0:512], ALU.add),
                        reads=[bb, l2acc], writes=[l2acc], sem="dv")
            p.emit("s", lambda e: e.activation(
                l2acc.h[:], l2acc.h[:], AF.Sqrt, bias=epsc.h[0:4, 1:2]),
                reads=[l2acc, epsc], writes=[l2acc], sem="ac")
            p.emit("v", [
                lambda e: e.reciprocal(l2acc.h[:], l2acc.h[:]),
                lambda e: e.tensor_copy(l2accb.h[:], l2acc.h[:]),
            ], reads=[l2acc], writes=[l2acc, l2accb], sem="dv")
            for ch in range(NCH):
                h = ch // 2
                bb = pnext()
                mmgrp(bb, bb.h[:, 0:512],
                      [(selh.h[:, h * 128:(h + 1) * 128], l2accb.h[0:4, :])],
                      [selh, l2accb])
                xap = fmv(buf, stride, off, ch, 0, 512)
                p.emit("v", lambda e, a=xap, b=bb: e.tensor_tensor(
                    a, a, b.h[:, 0:512], ALU.mult),
                    reads=[bb, buf], writes=[buf], sem="dv")

        def emit_small_proj(lhs_off, act, bias_ap, dstbuf, xnf):
            bank = pnext()
            items = [(wsmall.h[:, k * 8 + lhs_off: k * 8 + lhs_off + 4],
                      xnf.h[:, k * 512:(k + 1) * 512]) for k in range(NCH)]
            mmgrp(bank, bank.h[0:4, 0:512], items, [wsmall, xnf])
            if bias_ap is None:
                p.emit("s", lambda e: e.activation(
                    dstbuf.h[:], bank.h[0:4, 0:512], act),
                    reads=[bank, smalls], writes=[dstbuf], sem="ac")
            else:
                p.emit("s", lambda e: e.activation(
                    dstbuf.h[:], bank.h[0:4, 0:512], act, bias=bias_ap),
                    reads=[bank, smalls], writes=[dstbuf], sem="ac")

        def gen_proj_conv(wd, dst, dstride, doff, wname, xnf):
            halo = zhalo[wname[-1]]
            for cbi in range(2):
                wb_ = wload(wd.rearrange("(k p) m -> p k m", p=128)
                            [:, :, cbi * 512:(cbi + 1) * 512])
                for mloc in range(4):
                    ch = cbi * 4 + mloc
                    zp = zping[ch % 2]
                    bank = pnext()
                    items = [(wb_.h[:, k * 512 + mloc * 128:
                                    k * 512 + mloc * 128 + 128],
                              xnf.h[:, k * 512:(k + 1) * 512])
                             for k in range(NCH)]
                    mmgrp(bank, bank.h[:, 0:512], items, [wb_, xnf])
                    p.emit("s", lambda e, b=bank, z=zp: e.activation(
                        z.h[:, 3:515], b.h[:, 0:512], AF.Identity),
                        reads=[bank], writes=[zp], sem="ac")
                    p.emit("v", lambda e, z=zp, hl=halo, ch=ch: e.tensor_copy(
                        z.h[:, 0:3], hl.h[:, ch * 3:(ch + 1) * 3]),
                        reads=[halo], writes=[zp], sem="dv")
                    p.emit("v", lambda e, z=zp, hl=halo, ch=ch: e.tensor_copy(
                        hl.h[:, ch * 3:(ch + 1) * 3], z.h[:, 512:515]),
                        reads=[zp], writes=[halo], sem="dv")
                    w0, w1, w2, w3 = (scol(wname, ch * 4 + i, 1) for i in range(4))
                    p.emit("v", [
                        lambda e, z=zp, w=w0: e.tensor_scalar(
                            cvp[0].h[:], z.h[:, 0:512], w, None, ALU.mult),
                        lambda e, z=zp, w=w1: e.scalar_tensor_tensor(
                            cvp[1].h[:], z.h[:, 1:513], w, cvp[0].h[:],
                            ALU.mult, ALU.add),
                        lambda e, z=zp, w=w2: e.scalar_tensor_tensor(
                            cvp[0].h[:], z.h[:, 2:514], w, cvp[1].h[:],
                            ALU.mult, ALU.add),
                        lambda e, z=zp, w=w3: e.scalar_tensor_tensor(
                            cvp[1].h[:], z.h[:, 3:515], w, cvp[0].h[:],
                            ALU.mult, ALU.add),
                    ], reads=[zp, smalls], writes=[cvp[0], cvp[1]], sem="dv")
                    p.emit("s", lambda e, d=fmv(dst, dstride, doff, ch, 0, 512):
                           e.activation(d, cvp[1].h[:], AF.Silu),
                           reads=[cvp[1]], writes=[dst], sem="ac")
                    yield

        def gen_head(t):
            xnf = xn_fm2[t % 2]
            for s_ in range(4):
                xr = x_sub[0]
                p.emit("y", lambda e, t=t, s_=s_, d=xr.h: e.dma_start(
                    d.rearrange("p (o dd) -> p o dd", o=1),
                    x_in.rearrange("(u p) d -> p u d", p=128)
                    [:, t * 4 + s_:t * 4 + s_ + 1, :]),
                    writes=[xr], sem="dx0", dma=True)
                ln_norm(xr, 0, xn_tm, (s_ % 2) * 1024, s_)
                tm_to_fm_sub(xn_tm, (s_ % 2) * 1024, xnf, s_ * 128)
                yield
            emit_small_proj(0, AF.Sigmoid, None, beta_f, xnf)
            fns = [(lambda e, c=c: e.transpose(
                PF[5].h[0:128, c * 4:(c + 1) * 4],
                beta_f.h[0:4, c * 128:(c + 1) * 128], ident_f32.h[0:4, 0:4]))
                for c in range(4)]
            p.emit("t", fns, reads=[beta_f, ident_f32], writes=[PF[5]], sem="pe")
            p.emit("v", lambda e: e.tensor_copy(beta_tm.h[:], PF[5].h[:, 0:16]),
                   reads=[PF[5]], writes=[beta_tm], sem="dv")
            yield
            emit_small_proj(4, AF.Sigmoid, scol("bid")[0:4, :], idg, xnf)
            p.emit("v", lambda e: e.tensor_scalar(
                idsc_bf.h[:], idg.h[:], sal.h[0:4, :], 0.06, ALU.mult, ALU.add),
                reads=[idg, sal], writes=[idsc_bf], sem="dv")
            yield
            yield from gen_proj_conv(wq_d, q_f, 512, 0, "conv_q", xnf)
            emit_l2norm_tile(q_f, 512, 0)
            yield
            yield from gen_proj_conv(wk_d, k_f, 512, 0, "conv_k", xnf)
            emit_l2norm_tile(k_f, 512, 0)
            yield
            yield from gen_proj_conv(wv_d, v_f2[t % 2], 542, 30, "conv_v", xnf)

        def gen_fir(t, ch, df):
            vf = v_f2[t % 2]
            base = ch * 542 + 30
            for (dst, wname, T) in ((fs_f, "fir_s", 3), (fl_f, "fir_l", 31)):
                for i0_ in range(0, T, 8):
                    fns = []
                    for i in range(i0_, min(i0_ + 8, T)):
                        w = scol(wname, ch * T + i, 1)
                        srcv = vf.h[:, base - (T - 1) + i:
                                    base - (T - 1) + i + 512]
                        if i == 0:
                            fns.append(lambda e, s2=srcv, w2=w: e.tensor_scalar(
                                firv[1].h[:], s2, w2, None, ALU.mult))
                        else:
                            prev = firv[i % 2]
                            d2 = firv[(i + 1) % 2].h[:] if i < T - 1 else                                 fmv(dst, 512, 0, ch, 0, 512)
                            fns.append(lambda e, s2=srcv, w2=w, pr=prev, d2=d2:
                                       e.scalar_tensor_tensor(
                                           d2, s2, w2, pr.h[:],
                                           ALU.mult, ALU.add))
                    p.emit("v", fns, reads=[vf, smalls],
                           writes=[firv[0], firv[1], dst], sem="dv")
                    yield

        def gen_delta(t, c, h, pipe, df):
            D = dset[pipe]
            PBp = PB if pipe == 0 else PB2
            DPOOL = (0, 1, 2, 3, 4, 5)
            c0 = c * 128
            qv = [fmv(q_f, 512, 0, 2 * h + u, c0, 128) for u in range(2)]
            kv = [fmv(k_f, 512, 0, 2 * h + u, c0, 128) for u in range(2)]
            vf = v_f2[t % 2]
            vv = [fmv(vf, 542, 30, 2 * h + u, c0, 128) for u in range(2)]
            bcol = beta_tm.h[:, c * 4 + h: c * 4 + h + 1]
            Sb = Sbfh[h]
            S3 = S32h[h]

            b5 = pnext(DPOOL)
            mmgrp(b5, b5.h[:, 0:128],
                  [(kv[u], kv[u]) for u in range(2)], [k_f])
            yield
            p.emit("v", lambda e, b=b5: e.scalar_tensor_tensor(
                D["acu"].h[:], b.h[:, 0:128], bcol, maskLneg.h[:],
                ALU.mult, ALU.mult),
                reads=[b5, beta_tm, maskLneg], writes=[D["acu"]], sem="dv")
            yield
            fns = [(lambda e, i=kv[u], o=PBp.h[:, u * 128:(u + 1) * 128]:
                    e.transpose(o, i, ident_bf.h[:])) for u in range(2)]
            fns += [(lambda e, i=vv[u], o=PBp.h[:, 256 + u * 128:256 + (u + 1) * 128]:
                     e.transpose(o, i, ident_bf.h[:])) for u in range(2)]
            fns.append(lambda e: e.transpose(
                PBp.h[:, 768:896], D["acu"].h[:], ident_bf.h[:]))
            p.emit("t", fns, reads=[k_f, vf, D["acu"], ident_bf],
                   writes=[PBp], sem="pe")
            yield
            p.emit("v", lambda e: e.tensor_copy(D["knt"].h[:], PBp.h[:, 0:256]),
                   reads=[PBp], writes=[D["knt"]], sem="dv")
            p.emit("v", lambda e: e.tensor_scalar(
                D["vbt"].h[:], PBp.h[:, 256:512], bcol, None, ALU.mult),
                reads=[PBp, beta_tm], writes=[D["vbt"]], sem="dv")
            p.emit("v", lambda e: e.tensor_copy(D["at"].h[:], PBp.h[:, 768:896]),
                reads=[PBp], writes=[D["at"]], sem="dv")
            yield
            p.emit("v", lambda e: e.tensor_scalar(
                D["kbt"].h[:], D["knt"].h[:], bcol, None, ALU.mult),
                reads=[D["knt"], beta_tm], writes=[D["kbt"]], sem="dv")
            p.emit("v", lambda e: e.tensor_tensor(
                D["ra"].h[:], ident_bf.h[:], D["at"].h[:], ALU.add),
                reads=[ident_bf, D["at"]], writes=[D["ra"]], sem="dv")
            yield

            cur, curt = D["acu"], D["at"]
            nslots = [D["acu"], D["ev1"]]
            tslots = [D["acut"], D["evt"]]
            Rcur, Rnxt = D["ra"], D["rb"]
            for lvl in range(1, 7):
                new = nslots[lvl % 2]
                bA = pnext(DPOOL)
                mmgrp(bA, bA.h[:, 0:128], [(curt.h[:], cur.h[:])], [cur, curt])
                ev_eng = "v" if lvl % 2 else "s"
                p.emit(ev_eng, lambda e, b=bA, o=new: (
                    e.tensor_copy(o.h[:], b.h[:, 0:128]) if hasattr(e, "tensor_copy")
                    else e.copy(o.h[:], b.h[:, 0:128])),
                    reads=[bA], writes=[new],
                    sem="dv" if ev_eng == "v" else "ac")
                yield
                if lvl < 6:
                    newt = tslots[lvl % 2]
                    bB = pnext(DPOOL)
                    mmgrp(bB, bB.h[:, 0:128], [(cur.h[:], curt.h[:])], [cur, curt])
                    p.emit("s", lambda e, b=bB, o=newt: e.copy(
                        o.h[:], b.h[:, 0:128]),
                        reads=[bB], writes=[newt], sem="ac")
                    yield
                else:
                    newt = None
                bC = pnext(DPOOL)
                mmgrp(bC, bC.h[:, 0:128], [(new.h[:], Rcur.h[:])], [new, Rcur])
                p.emit("v", lambda e, b=bC, rc=Rcur, rn=Rnxt: e.tensor_tensor(
                    rn.h[:], rc.h[:], b.h[:, 0:128], ALU.add),
                    reads=[bC, Rcur], writes=[Rnxt], sem="dv")
                yield
                Rcur, Rnxt = Rnxt, Rcur
                cur, curt = new, newt

            b6 = pnext(DPOOL)
            fns = [(lambda e, l=D["kbt"].h[:, uu * 128:(uu + 1) * 128],
                    o=b6.h[:, uu * 128:(uu + 1) * 128], r=Rcur.h[:]:
                    e.matmul(o, l, r, start=True, stop=True)) for uu in range(2)]
            p.emit("t", fns, reads=[D["kbt"], Rcur], writes=[b6], sem="pe")
            yield
            p.emit("s", lambda e, b=b6: e.copy(D["wfm"].h[:], b.h[:, 0:256]),
                   reads=[b6], writes=[D["wfm"]], sem="ac")
            yield
            b7 = pnext(DPOOL)
            mmgrp(b7, b7.h[:, 0:256],
                  [(D["wfm"].h[:, kc * 128:(kc + 1) * 128],
                    S_bf.h[:, (h * 2 + kc) * 256:(h * 2 + kc + 1) * 256])
                   for kc in range(2)], [D["wfm"], Sb])
            yield
            p.emit("s", lambda e, b=b7: e.copy(D["ws"].h[:], b.h[:, 0:256]),
                   reads=[b7], writes=[D["ws"]], sem="ac")
            yield
            b8 = pnext(DPOOL)
            mmgrp(b8, b8.h[:, 0:256], [(Rcur.h[:], D["vbt"].h[:])],
                  [Rcur, D["vbt"]])
            yield
            p.emit("v", lambda e, b=b8: e.tensor_tensor(
                D["u2"].h[:], b.h[:, 0:256], D["ws"].h[:], ALU.subtract),
                reads=[b8, D["ws"]], writes=[D["u2"]], sem="dv")
            yield
            b9 = pnext(DPOOL)
            mmgrp(b9, b9.h[:, 0:128],
                  [(kv[u], qv[u]) for u in range(2)], [k_f, q_f])
            yield
            p.emit("v", lambda e, b=b9: e.tensor_tensor(
                D["att"].h[:], maskUincl.h[:], b.h[:, 0:128], ALU.mult),
                reads=[b9, maskUincl], writes=[D["att"]], sem="dv")
            yield
            b10 = pnext(DPOOL)
            items = [(qv[kc],
                      S_bf.h[:, (h * 2 + kc) * 256:(h * 2 + kc + 1) * 256])
                     for kc in range(2)]
            items.append((D["att"].h[:], D["u2"].h[:]))
            mmgrp(b10, b10.h[:, 0:256], items, [q_f, Sb, D["att"], D["u2"]])
            yield
            p.emit("v", lambda e, b=b10: e.tensor_copy(D["otm"].h[:],
                                                       b.h[:, 0:256]),
                   reads=[b10], writes=[D["otm"]], sem="dv")
            yield
            fns = [(lambda e, i=D["otm"].h[:, u * 128:(u + 1) * 128],
                    o=PBp.h[:, 512 + u * 128:512 + (u + 1) * 128]:
                    e.transpose(o, i, ident_bf.h[:])) for u in range(2)]
            p.emit("t", fns, reads=[D["otm"], ident_bf], writes=[PBp], sem="pe")
            yield
            dv = df.h.rearrange("p (k c) -> p k c", k=NCH)[
                :, 2 * h:2 * h + 2, c0:c0 + 128]
            sv = PBp.h.rearrange("p (k c) -> p k c", k=NCH)[:, 4:6, :]
            p.emit("v", lambda e, dv=dv, sv=sv: e.tensor_copy(dv, sv),
                   reads=[PBp], writes=[df], sem="dv")
            yield
            for uu in range(2):
                b11 = pnext(DPOOL)
                mmgrp(b11, b11.h[:, 0:256],
                      [(D["knt"].h[:, uu * 128:(uu + 1) * 128], D["u2"].h[:])],
                      [D["knt"], D["u2"]])
                yield
                ssl = S_f32.h[:, (h * 2 + uu) * 256:(h * 2 + uu + 1) * 256]
                p.emit("v", lambda e, b=b11, s2=ssl: e.tensor_tensor(
                    s2, s2, b.h[:, 0:256], ALU.add),
                    reads=[b11, S3], writes=[S3], sem="dv")
                p.emit("s", lambda e, s2=ssl,
                       d2=S_bf.h[:, (h * 2 + uu) * 256:(h * 2 + uu + 1) * 256]:
                       e.copy(d2, s2),
                       reads=[S3], writes=[Sb], sem="ac")
                yield

        def gen_dfir(t):
            df = delta_f2[t % 2]

            def gen_fir_pair(c):
                yield from gen_fir(t, 2 * c, df)
                yield from gen_fir(t, 2 * c + 1, df)

            def gen_delta_chain(c, pipe):
                yield from gen_delta(t, c, pipe, pipe, df)
                yield from gen_delta(t, c, pipe + 2, pipe, df)

            for c in range(4):
                sub = [gen_fir_pair(c), gen_delta_chain(c, 0),
                       gen_delta_chain(c, 1)]
                while sub:
                    keep = []
                    for g_ in sub:
                        try:
                            next(g_)
                            keep.append(g_)
                        except StopIteration:
                            pass
                    sub = keep
                    yield

        def gen_tail_a(t):
            xnf = xn_fm2[t % 2]
            df = delta_f2[t % 2]
            vf = v_f2[t % 2]
            for si, X in enumerate((fs_f, fl_f, df)):
                for j in range(NCH):
                    xap = fmv(X, 512, 0, j, 0, 512)
                    p.emit("s", lambda e, a=xap, o=sqbf[j % 2]: e.activation(
                        o.h[:], a, AF.Square),
                        reads=[X], writes=[sqbf[j % 2]], sem="ac")
                    lhs = statred.h[:, (si * 8 + j) * 12:(si * 8 + j + 1) * 12]
                    s0 = (si == 0 and j == 0)
                    s1 = (si == 2 and j == 7)
                    p.emit("t", lambda e, l=lhs, r=xap, s0=s0, s1=s1: e.matmul(
                        PF[0].h[0:12, 0:512], l, r, start=s0, stop=s1),
                        reads=[X, statred], writes=[PF[0]], sem="pe")
                    p.emit("t", lambda e, l=lhs, r=sqbf[j % 2].h[:], s0=s0, s1=s1:
                           e.matmul(PF[1].h[0:12, 0:512], l, r, start=s0, stop=s1),
                           reads=[sqbf[j % 2], statred], writes=[PF[1]], sem="pe")
            p.emit("s", lambda e: e.copy(stat_m.h[:], PF[0].h[0:12, 0:512]),
                   reads=[PF[0]], writes=[stat_m], sem="ac")
            p.emit("s", lambda e: e.copy(stat_q.h[:], PF[1].h[0:12, 0:512]),
                   reads=[PF[1]], writes=[stat_q], sem="ac")
            yield
            p.emit("v", [
                lambda e: e.tensor_tensor(stat_v.h[:], stat_m.h[:], stat_m.h[:],
                                          ALU.mult),
                lambda e: e.tensor_tensor(stat_q.h[:], stat_q.h[:], stat_v.h[:],
                                          ALU.subtract),
            ], reads=[stat_m, stat_q], writes=[stat_v, stat_q], sem="dv")
            p.emit("s", lambda e: e.activation(stat_v.h[:], stat_q.h[:], AF.Sqrt),
                   reads=[stat_q], writes=[stat_v], sem="ac")
            p.emit("v", [
                lambda e: e.tensor_copy(stat_mbf.h[:], stat_m.h[:]),
                lambda e: e.tensor_copy(stat_sbf.h[:], stat_v.h[:]),
            ], reads=[stat_m, stat_v], writes=[stat_mbf, stat_sbf], sem="dv")
            yield
            for cbi in range(4):
                wb_ = wload(wr1x_d.rearrange("(k p) m -> p k m", p=128)
                            [:, :, cbi * 512:(cbi + 1) * 512])
                for mloc in range(4):
                    m = cbi * 4 + mloc
                    bank = pnext(pool=(0, 1, 5))
                    items = [(wb_.h[:, k * 512 + mloc * 128:
                                    k * 512 + mloc * 128 + 128],
                              xnf.h[:, k * 512:(k + 1) * 512])
                             for k in range(NCH)]
                    items.append((wr1st[0].h[:, m * 128:(m + 1) * 128],
                                  stat_mbf.h[:]))
                    items.append((wr1st[1].h[:, m * 128:(m + 1) * 128],
                                  stat_sbf.h[:]))
                    mmgrp(bank, bank.h[:, 0:512], items,
                          [wb_, xnf, wr1st[0], wr1st[1], stat_mbf, stat_sbf])
                    p.emit("s", lambda e, b=bank, m=m, o=hm_bf[0]:
                           e.activation(o.h[:], b.h[:, 0:512], AF.Gelu,
                                        bias=scol("br1", m, 1)),
                           reads=[bank, smalls], writes=[hm_bf[0]], sem="ac")
                    for cc in range(3):
                        p.emit("t", lambda e, cc=cc, m=m, mloc=mloc: e.matmul(
                            PF[2 + cc].h[0:4, 0:512],
                            wr2cs[cc].h[:, m * 4:(m + 1) * 4],
                            hm_bf[0].h[:], start=(mloc == 0),
                            stop=(mloc == 3)),
                            reads=[hm_bf[0], wr2cs[cc]],
                            writes=[PF[2 + cc]], sem="pe")
                for cc in range(3):
                    if cbi == 0:
                        p.emit("v", lambda e, cc=cc: e.tensor_copy(
                            lg[cc].h[:], PF[2 + cc].h[0:4, 0:512]),
                            reads=[PF[2 + cc]], writes=[lg[cc]], sem="dv")
                    else:
                        p.emit("v", lambda e, cc=cc: e.tensor_tensor(
                            lg[cc].h[:], lg[cc].h[:], PF[2 + cc].h[0:4, 0:512],
                            ALU.add),
                            reads=[PF[2 + cc], lg[cc]], writes=[lg[cc]], sem="dv")
                yield
            for cc in range(3):
                p.emit("v", lambda e, cc=cc: e.tensor_scalar(
                    lg[cc].h[:], lg[cc].h[:], scol(f"br2_{cc}")[0:4, 0:1], None,
                    ALU.add),
                    reads=[lg[cc], smalls], writes=[lg[cc]], sem="dv")
            ivc = invt.h[0:4, :]
            p.emit("v", [
                lambda e: e.tensor_scalar(ebx[0].h[:], lg[0].h[:], ivc, None, ALU.mult),
                lambda e: e.tensor_scalar(ebx[1].h[:], lg[1].h[:], ivc, None, ALU.mult),
                lambda e: e.tensor_scalar(ebx[2].h[:], lg[2].h[:], ivc, None, ALU.mult),
                lambda e: e.tensor_tensor(mxa.h[:], ebx[0].h[:], ebx[1].h[:], ALU.max),
                lambda e: e.tensor_tensor(mxa.h[:], mxa.h[:], ebx[2].h[:], ALU.max),
                lambda e: e.tensor_tensor(lg[0].h[:], ebx[0].h[:], mxa.h[:], ALU.subtract),
                lambda e: e.tensor_tensor(lg[1].h[:], ebx[1].h[:], mxa.h[:], ALU.subtract),
                lambda e: e.tensor_tensor(lg[2].h[:], ebx[2].h[:], mxa.h[:], ALU.subtract),
            ], reads=[lg[0], lg[1], lg[2], invt],
               writes=[ebx[0], ebx[1], ebx[2], mxa, lg[0], lg[1], lg[2]],
               sem="dv")
            for cc in range(3):
                p.emit("s", lambda e, cc=cc: e.activation(
                    ebx[cc].h[:], lg[cc].h[:], AF.Exp),
                    reads=[lg[cc]], writes=[ebx[cc]], sem="ac")
            p.emit("v", [
                lambda e: e.tensor_tensor(sma.h[:], ebx[0].h[:], ebx[1].h[:], ALU.add),
                lambda e: e.tensor_tensor(sma.h[:], sma.h[:], ebx[2].h[:], ALU.add),
                lambda e: e.reciprocal(sma.h[:], sma.h[:]),
                lambda e: e.tensor_tensor(lg[0].h[:], ebx[0].h[:], sma.h[:], ALU.mult),
                lambda e: e.tensor_tensor(lg[1].h[:], ebx[1].h[:], sma.h[:], ALU.mult),
                lambda e: e.tensor_tensor(lg[2].h[:], ebx[2].h[:], sma.h[:], ALU.mult),
                lambda e: e.tensor_scalar(pp[0].h[:], lg[0].h[:], 0.925, 0.025,
                                          ALU.mult, ALU.add),
                lambda e: e.tensor_scalar(pp[1].h[:], lg[1].h[:], 0.925, 0.025,
                                          ALU.mult, ALU.add),
                lambda e: e.tensor_scalar(pp[2].h[:], lg[2].h[:], 0.925, 0.025,
                                          ALU.mult, ALU.add),
            ], reads=[ebx[0], ebx[1], ebx[2], lg[0], lg[1], lg[2]],
               writes=[sma, lg[0], lg[1], lg[2], pp[0], pp[1], pp[2]],
               sem="dv")
            yield
            for ch in range(NCH):
                h = ch // 2
                srcs = [(pp[0], fmv(fs_f, 512, 0, ch, 0, 512), fs_f),
                        (pp[1], fmv(fl_f, 512, 0, ch, 0, 512), fl_f),
                        (pp[2], fmv(df, 512, 0, ch, 0, 512), df),
                        (idsc_bf, fmv(vf, 542, 30, ch, 0, 512), vf)]
                for i, (pv, xv, xb) in enumerate(srcs):
                    bb = pnext()
                    mmgrp(bb, bb.h[:, 0:512],
                          [(selh.h[:, h * 128:(h + 1) * 128], pv.h[0:4, :])],
                          [selh, pv])
                    p.emit("v", lambda e, b=bb, a=xv, o=cmb[i]: e.tensor_tensor(
                        o.h[:], b.h[:, 0:512], a, ALU.mult),
                        reads=[bb, xb], writes=[cmb[i]], sem="dv")
                p.emit("g", lambda e: e.tensor_tensor(
                    oc[0].h[:], cmb[0].h[:], cmb[1].h[:], ALU.add),
                    reads=[cmb[0], cmb[1]], writes=[oc[0]], sem="po")
                p.emit("g", lambda e: e.tensor_tensor(
                    cmb[0].h[:], cmb[2].h[:], cmb[3].h[:], ALU.add),
                    reads=[cmb[2], cmb[3], cmb[0]], writes=[cmb[0]], sem="po")
                p.emit("v",
                    lambda e, d=fmv(o_comb, 512, 0, ch, 0, 512): e.tensor_tensor(
                        d, oc[0].h[:], cmb[0].h[:], ALU.add),
                    reads=[oc[0], cmb[0]], writes=[o_comb], sem="dv")
                if ch % 2 == 1:
                    yield

        def gen_tail_b(t):
            xnf = xn_fm2[t % 2]
            df = delta_f2[t % 2]
            ao = df
            for ch in range(NCH):
                xap = fmv(o_comb, 512, 0, ch, 0, 512)
                p.emit("s", lambda e, a=xap, o=sqbf[ch % 2]: e.activation(
                    o.h[:], a, AF.Square),
                    reads=[o_comb], writes=[sqbf[ch % 2]], sem="ac")
                p.emit("t", lambda e, ch=ch: e.matmul(
                    PF[0].h[0:4, 0:512], rmsred.h[:, ch * 4:(ch + 1) * 4],
                    sqbf[ch % 2].h[:], start=(ch == 0), stop=(ch == 7)),
                    reads=[sqbf[ch % 2], rmsred], writes=[PF[0]], sem="pe")
            p.emit("s", lambda e: e.activation(
                rms_r.h[:], PF[0].h[0:4, 0:512], AF.Sqrt, bias=epsc.h[0:4, 0:1]),
                reads=[PF[0], epsc], writes=[rms_r], sem="ac")
            p.emit("v", [
                lambda e: e.reciprocal(rms_r.h[:], rms_r.h[:]),
                lambda e: e.tensor_copy(rms_rb.h[:], rms_r.h[:]),
            ], reads=[rms_r], writes=[rms_r, rms_rb], sem="dv")
            yield
            for ch in range(NCH):
                h = ch // 2
                bb = pnext()
                mmgrp(bb, bb.h[:, 0:512],
                      [(selh.h[:, h * 128:(h + 1) * 128], rms_rb.h[0:4, :])],
                      [selh, rms_rb])
                ocv = fmv(o_comb, 512, 0, ch, 0, 512)
                onv = xnf.h[:, ch * 512:(ch + 1) * 512]
                wcol = scol("o_norm", ch % 2, 1)
                p.emit("v", [
                    lambda e, b=bb, a=ocv: e.tensor_tensor(
                        cmb[0].h[:], a, b.h[:, 0:512], ALU.mult),
                    lambda e, o2=onv, w=wcol: e.tensor_scalar(
                        o2, cmb[0].h[:], w, None, ALU.mult),
                ], reads=[bb, o_comb, smalls], writes=[cmb[0], xnf], sem="dv")
                if ch % 2 == 1:
                    yield
            for cbi in range(2):
                wb_ = wload(wo_d.rearrange("(k p) m -> p k m", p=128)
                            [:, :, cbi * 512:(cbi + 1) * 512])
                for mloc in range(4):
                    ch = cbi * 4 + mloc
                    bank = pnext()
                    items = [(wb_.h[:, k * 512 + mloc * 128:
                                    k * 512 + mloc * 128 + 128],
                              xnf.h[:, k * 512:(k + 1) * 512])
                             for k in range(NCH)]
                    mmgrp(bank, bank.h[:, 0:512], items, [wb_, xnf])
                    p.emit("s", lambda e, b=bank, d=fmv(ao, 512, 0, ch, 0, 512):
                           e.activation(d, b.h[:, 0:512], AF.Identity),
                           reads=[bank], writes=[ao], sem="ac")
                    yield
            for s_ in range(4):
                xr = x_sub[0]
                p.emit("y", lambda e, t=t, s_=s_, d=xr.h: e.dma_start(
                    d.rearrange("p (o dd) -> p o dd", o=1),
                    x_in.rearrange("(u p) d -> p u d", p=128)
                    [:, t * 4 + s_:t * 4 + s_ + 1, :]),
                    writes=[xr], sem="dx0", dma=True)
                fns = []
                for m in range(NCH):
                    inap = ao.h[:, m * 512 + s_ * 128: m * 512 + s_ * 128 + 128]
                    outap = PB.h[:, m * 128:(m + 1) * 128]
                    fns.append(lambda e, i=inap, o=outap:
                               e.transpose(o, i, ident_bf.h[:]))
                p.emit("t", fns, reads=[ao, ident_bf], writes=[PB], sem="pe")
                p.emit("v", lambda e, x0=xr: e.tensor_tensor(
                    x0.h[:], x0.h[:], PB.h[:, 0:1024], ALU.add),
                    reads=[xr, PB], writes=[xr], sem="dv")
                ln_norm(xr, 0, xn_tm, (s_ % 2) * 1024, 4 + s_)
                p.emit("y", lambda e, t=t, s_=s_, x1=xr: e.dma_start(
                    x1s.rearrange("(u p) d -> p u d", p=128)
                    [:, t * 4 + s_:t * 4 + s_ + 1, :],
                    x1.h.rearrange("p (o dd) -> p o dd", o=1)),
                    reads=[xr], sem="dxa0", dma=True)
                tm_to_fm_sub(xn_tm, (s_ % 2) * 1024, xnf, s_ * 128)
                yield
            p.emit("y", lambda e, t=t, xnf=xnf: e.dma_start(
                xn1s.rearrange("(k p) tt -> p k tt", p=128)
                [:, :, t * 512:(t + 1) * 512],
                xnf.h.rearrange("p (k c) -> p k c", k=NCH)),
                reads=[xnf], sem="dxn1", dma=True)

        def run_gens(gens, until=None):
            need = set(until if until is not None else gens)
            alive = list(gens)
            while need & set(alive):
                keep = []
                for g_ in alive:
                    try:
                        next(g_)
                        keep.append(g_)
                    except StopIteration:
                        pass
                alive = keep
            return alive

        run_gens([gen_head(0)])
        lv = []
        for t in range(NTILE):
            dg = gen_dfir(t)
            lv = run_gens([dg] + lv, until=[dg])
            run_gens(lv)
            lv = []
            if t + 1 < NTILE:
                vsrc, vdst = v_f2[t % 2], v_f2[(t + 1) % 2]
                p.emit("v", lambda e, a=vsrc, b=vdst: e.tensor_copy(
                    b.h.rearrange("p (k c) -> p k c", k=8)[:, :, 0:30],
                    a.h.rearrange("p (k c) -> p k c", k=8)[:, :, 512:542]),
                    reads=[vsrc], writes=[vdst], sem="dv")
            gens = [gen_tail_a(t)]
            if t + 1 < NTILE:
                gens.append(gen_head(t + 1))
            run_gens(gens)
            lv = [gen_tail_b(t)]
        run_gens(lv)

    else:
        def emit_tile_mini(t):
            for s_ in range(4):
                xr = x_sub[0]
                p.emit("y", lambda e, t=t, s_=s_, d=xr.h: e.dma_start(
                    d.rearrange("p (o dd) -> p o dd", o=1),
                    x_in.rearrange("(u p) d -> p u d", p=128)
                    [:, t * 4 + s_:t * 4 + s_ + 1, :]),
                    writes=[xr], sem="dx0", dma=True)
                ln_norm(xr, 0, xn_tm, (s_ % 2) * 1024, s_)
                tm_to_fm_sub(xn_tm, (s_ % 2) * 1024, xn_fm2[0], s_ * 128)
                p.emit("y", lambda e, t=t, s_=s_, x1=xr: e.dma_start(
                    x1s.rearrange("(u p) d -> p u d", p=128)
                    [:, t * 4 + s_:t * 4 + s_ + 1, :],
                    x1.h.rearrange("p (o dd) -> p o dd", o=1)),
                    reads=[xr], sem="dxa0", dma=True)
            p.emit("y", lambda e, t=t: e.dma_start(
                xn1s.rearrange("(k p) tt -> p k tt", p=128)
                [:, :, t * 512:(t + 1) * 512],
                xn_fm2[0].h.rearrange("p (k c) -> p k c", k=NCH)),
                reads=[xn_fm2[0]], sem="dxn1", dma=True)

        for t in range(NTILE):
            emit_tile_mini(t)

    p.barrier()
    ph1.close()

    xn1 = sb("xn1", (128, 8 * 1024), BF16)
    x1tm = sb("x1tm", (128, 8 * 1024), F32)
    hbuf = sb("hbuf", (128, 32 * 1024), BF16)
    oacc = sb("oacc", (128, 8 * 1024), BF16)
    w1b = [sb(f"w1b{i}", (128, 8 * 512), BF16) for i in range(2)]
    w2b = [sb(f"w2b{i}", (128, 32 * 256), BF16) for i in range(2)]
    tmpy = [sb(f"tmpy{i}", (128, 512), BF16) for i in range(2)]
    mb_sb = sb("mb_sb", (128, 1024), mybir.dt.uint8)
    wredb = sb("wredb", (128, 8 * 16), BF16)
    wgf = sb("wgf", (8, 16), F32)
    wgn = sb("wgn", (8, 16), BF16)
    wgsc = sb("wgsc", (8, 4), F32)
    wgnT = sb("wgnT", (16, 8), BF16)
    b1r = sb("b1r_s", (128, 8 * 32), F32)
    b2r = sb("b2r_s", (128, 8 * 8), F32)
    red_bf = sb("red_bf", (16, 1024), BF16)
    lgf = sb("lgf", (8, 1024), F32)
    ltm = sb("ltm", (128, 64), F32)
    mask_tm = sb("mask_tm", (128, 64), BF16)
    extm = sb("extm", (128, 8), F32)
    mask_fm = sb("mask_fm", (8, 1024), BF16)
    scores_tm = sb("scores_tm", (128, 8), F32)
    rt_sc = sb("rt_sc", (128, 2), F32)
    out_tm = [sb(f"out_tm{i}", (128, 1024), F32) for i in range(2)]

    p.emit("g", lambda e: e.dma_start(
        wredb.h.rearrange("p (k m) -> p k m", k=8),
        wred_d.rearrange("(k p) m -> p k m", p=128)),
        writes=[wredb], sem="ds2g", dma=True)
    p.emit("y", lambda e: e.dma_start(wgf.h[:], wg_d.ap()),
           writes=[wgf], sem="ds2", dma=True)
    p.emit("y", lambda e: e.dma_start(
        b1r.h.rearrange("p (e m) -> p e m", e=8), b1_d.rearrange("e p m -> p e m")),
        writes=[b1r], sem="ds2", dma=True)
    p.emit("y", lambda e: e.dma_start(
        b2r.h.rearrange("p (e m) -> p e m", e=8), b2_d.rearrange("e p m -> p e m")),
        writes=[b2r], sem="ds2", dma=True)
    _t2 = Tok("ds2", p.counts["ds2"], None)
    for _b in (wgf, b1r, b2r):
        _b.w = _t2
    wredb.w = Tok("ds2g", p.counts["ds2g"], None)
    wgsq = sb("wgsq", (8, 16), F32)
    p.emit("v", [
        lambda e: e.tensor_tensor(wgsq.h[:], wgf.h[:], wgf.h[:], ALU.mult),
        lambda e: e.reduce_sum(wgsc.h[:, 0:1], wgsq.h[:], axis=AX),
    ], reads=[wgf], writes=[wgsq, wgsc], sem="dv")
    p.emit("s", lambda e: e.activation(wgsc.h[:, 1:2], wgsc.h[:, 0:1], AF.Sqrt),
           reads=[wgsc], writes=[wgsc], sem="ac")
    p.emit("v", [
        lambda e: e.tensor_single_scalar(wgsc.h[:, 1:2], wgsc.h[:, 1:2], 1e-4,
                                         ALU.max),
        lambda e: e.reciprocal(wgsc.h[:, 2:3], wgsc.h[:, 1:2]),
        lambda e: e.tensor_scalar(wgn.h[:], wgf.h[:], wgsc.h[:, 2:3], None,
                                  ALU.mult),
    ], reads=[wgf, wgsc], writes=[wgsc, wgn], sem="dv")
    p.emit("t", lambda e: e.transpose(PB.h[0:16, 0:8], wgn.h[:],
                                      ident_bf.h[0:8, 0:8]),
           reads=[wgn, ident_bf], writes=[PB], sem="pe")
    p.emit("v", lambda e: e.tensor_copy(wgnT.h[:], PB.h[0:16, 0:8]),
           reads=[PB], writes=[wgnT], sem="dv")

    w2load_i = [0]

    def emit_tb(tb):
        p.emit("y", lambda e, tb=tb: e.dma_start(
            x1tm.h.rearrange("p (s d) -> p s d", s=8),
            x1s.rearrange("(u p) d -> p u d", p=128)[:, tb * 8:(tb + 1) * 8, :]),
            writes=[x1tm], sem="dxt", dma=True)
        p.emit("y", lambda e, tb=tb: e.dma_start(
            xn1.h.rearrange("p (k c) -> p k c", k=8),
            xn1s.rearrange("(k p) tt -> p k tt", p=128)
            [:, :, tb * 1024:(tb + 1) * 1024]),
            writes=[xn1], sem="dxn", dma=True)
        p.emit("g", lambda e: e.memset(oacc.h[:], 0.0),
               writes=[oacc], sem="po")

        for half in range(2):
            items = [(wredb.h[:, k * 16:(k + 1) * 16],
                      xn1.h[:, k * 1024 + half * 512: k * 1024 + half * 512 + 512])
                     for k in range(NCH)]
            mmgrp(PF[5], PF[5].h[0:16, 0:512], items, [wredb, xn1])
            p.emit("s", lambda e, half=half: e.activation(
                red_bf.h[:, half * 512:(half + 1) * 512], PF[5].h[0:16, 0:512],
                AF.Identity),
                reads=[PF[5]], writes=[red_bf], sem="ac")
            mmgrp(PF[5], PF[5].h[0:8, 0:512],
                  [(wgnT.h[:], red_bf.h[:, half * 512:(half + 1) * 512])],
                  [wgnT, red_bf])
            p.emit("s", lambda e, half=half: e.activation(
                lgf.h[:, half * 512:(half + 1) * 512], PF[5].h[0:8, 0:512],
                AF.Identity),
                reads=[PF[5]], writes=[lgf], sem="ac")
        fns = [(lambda e, b=b: e.transpose(
            PF[4].h[:, b * 8:(b + 1) * 8], lgf.h[0:8, b * 128:(b + 1) * 128],
            ident_f32.h[0:8, 0:8])) for b in range(8)]
        p.emit("t", fns, reads=[lgf, ident_f32], writes=[PF[4]], sem="pe")
        p.emit("v", lambda e: e.tensor_copy(ltm.h[:], PF[4].h[:, 0:64]),
               reads=[PF[4]], writes=[ltm], sem="dv")
        for b in range(8):
            lv = ltm.h[:, b * 8:(b + 1) * 8]
            mx = rt_sc.h[:, 0:1]
            p.emit("v", [
                lambda e, lv=lv, mx=mx: e.reduce_max(mx, lv, axis=AX),
                lambda e, b=b, lv=lv, mx=mx: e.tensor_scalar(
                    mask_tm.h[:, b * 8:(b + 1) * 8], lv, mx, None, ALU.is_ge),
                lambda e, lv=lv, mx=mx: e.tensor_scalar(
                    extm.h[:], lv, mx, None, ALU.subtract),
            ], reads=[ltm], writes=[rt_sc, mask_tm, extm], sem="dv")
            p.emit("s", lambda e: e.activation(
                extm.h[:], extm.h[:], AF.Exp, accum_out=rt_sc.h[:, 1:2]),
                reads=[extm], writes=[extm, rt_sc], sem="ac")
            p.emit("v", lambda e, b=b: e.reciprocal(
                scores_tm.h[:, b:b + 1], rt_sc.h[:, 1:2]),
                reads=[rt_sc], writes=[scores_tm], sem="dv")
        fns = [(lambda e, b=b: e.transpose(
            PB.h[0:8, b * 128:(b + 1) * 128], mask_tm.h[:, b * 8:(b + 1) * 8],
            ident_bf.h[:])) for b in range(8)]
        p.emit("t", fns, reads=[mask_tm, ident_bf], writes=[PB], sem="pe")
        p.emit("v", lambda e: e.tensor_copy(mask_fm.h[:], PB.h[0:8, 0:1024]),
               reads=[PB], writes=[mask_fm], sem="dv")

        for ex in range(E):
            for half in range(2):
                mmgrp(PF[4], PF[4].h[:, 0:512],
                      [(sele.h[:, ex * 128:(ex + 1) * 128],
                        mask_fm.h[:, half * 512:(half + 1) * 512])],
                      [sele, mask_fm])
                p.emit("v", lambda e, half=half: e.tensor_copy(
                    mb_sb.h[:, half * 512:(half + 1) * 512], PF[4].h[:, 0:512]),
                    reads=[PF[4]], writes=[mb_sb], sem="dv")
            for g in range(8):
                wb_ = w1b[g % 2]
                p.emit("g", lambda e, ex=ex, g=g, d=wb_.h: e.dma_start(
                    d.rearrange("p (k m) -> p k m", k=8),
                    w1_d[ex].rearrange("(k p) m -> p k m", p=128)
                    [:, :, g * 512:(g + 1) * 512]),
                    writes=[wb_], sem=f"dwb{g % 2}", dma=True)
                for mloc in range(4):
                    fch = g * 4 + mloc
                    for half in range(2):
                        bank = PF[(fch * 2 + half) % 4]
                        items = [(wb_.h[:, k * 512 + mloc * 128:
                                        k * 512 + mloc * 128 + 128],
                                  xn1.h[:, k * 1024 + half * 512:
                                        k * 1024 + half * 512 + 512])
                                 for k in range(NCH)]
                        mmgrp(bank, bank.h[:, 0:512], items, [wb_, xn1])
                        p.emit("s", lambda e, b=bank, ex=ex, fch=fch, half=half:
                               e.activation(
                                   hbuf.h[:, fch * 1024 + half * 512:
                                          fch * 1024 + half * 512 + 512],
                                   b.h[:, 0:512], AF.Gelu,
                                   bias=b1r.h[:, ex * 32 + fch:ex * 32 + fch + 1]),
                               reads=[bank, b1r], writes=[hbuf], sem="ac")
            for g2 in range(4):
                wb2 = w2b[g2 % 2]
                for piece in range(2):
                    p.emit("g", lambda e, ex=ex, g2=g2, piece=piece, d=wb2.h:
                           e.dma_start(
                        d.rearrange("p (k m) -> p k m", k=32)
                        [:, piece * 16:(piece + 1) * 16, :],
                        w2_d[ex].rearrange("(k p) m -> p k m", p=128)
                        [:, piece * 16:(piece + 1) * 16,
                         g2 * 256:(g2 + 1) * 256]),
                        writes=[wb2], sem=f"dw2{g2 % 2}", dma=True)
                for m2l in range(2):
                    m2 = g2 * 2 + m2l
                    for half in range(2):
                        bank = PF[4 + (m2l * 2 + half) % 2]
                        items = [(wb2.h[:, kk * 256 + m2l * 128:
                                        kk * 256 + m2l * 128 + 128],
                                  hbuf.h[:, kk * 1024 + half * 512:
                                         kk * 1024 + half * 512 + 512])
                                 for kk in range(32)]
                        mmgrp(bank, bank.h[:, 0:512], items, [wb2, hbuf])
                        ty = tmpy[half]
                        p.emit("s", lambda e, b=bank, ex=ex, m2=m2, ty=ty:
                               e.activation(ty.h[:], b.h[:, 0:512], AF.Identity,
                                            bias=b2r.h[:, ex * 8 + m2:
                                                       ex * 8 + m2 + 1]),
                               reads=[bank, b2r], writes=[ty], sem="ac")
                        ov = oacc.h[:, m2 * 1024 + half * 512:
                                    m2 * 1024 + half * 512 + 512]
                        mv = mb_sb.h[:, half * 512:(half + 1) * 512]
                        p.emit("v", lambda e, ty=ty, ov=ov, mv=mv:
                               e.copy_predicated(ov, mv, ty.h[:]),
                               reads=[ty, mb_sb, oacc], writes=[oacc], sem="dv")
        for s_ in range(8):
            fns = [(lambda e, m2=m2, s_=s_: e.transpose(
                PB.h[:, m2 * 128:(m2 + 1) * 128],
                oacc.h[:, m2 * 1024 + s_ * 128: m2 * 1024 + s_ * 128 + 128],
                ident_bf.h[:])) for m2 in range(8)]
            p.emit("t", fns, reads=[oacc, ident_bf], writes=[PB], sem="pe")
            ot = out_tm[s_ % 2]
            p.emit("v", lambda e, s_=s_, ot=ot: e.scalar_tensor_tensor(
                ot.h[:], PB.h[:, 0:1024], scores_tm.h[:, s_:s_ + 1],
                x1tm.h[:, s_ * 1024:(s_ + 1) * 1024], ALU.mult, ALU.add),
                reads=[PB, scores_tm, x1tm], writes=[ot], sem="dv")
            p.emit("y", lambda e, tb=tb, s_=s_, ot=ot: e.dma_start(
                out_d.rearrange("(u p) d -> p u d", p=128)
                [:, tb * 8 + s_: tb * 8 + s_ + 1, :],
                ot.h.rearrange("p (o dd) -> p o dd", o=1)),
                reads=[ot], sem=f"do{s_ % 2}", dma=True)

    for tb in range(NTB):
        emit_tb(tb)

    p.barrier()

    with contextlib.ExitStack() as semstack:
        for s in sorted(p.counts.keys()):
            p.sems[s] = semstack.enter_context(nc.semaphore(s))
        with nc.Block() as block:
            @block.tensor
            def _(eng):
                for f in p.ops["t"]:
                    f(eng)

            @block.vector
            def _(eng):
                for f in p.ops["v"]:
                    f(eng)

            @block.scalar
            def _(eng):
                for f in p.ops["s"]:
                    f(eng)

            @block.gpsimd
            def _(eng):
                for f in p.ops["g"]:
                    f(eng)

            @block.sync
            def _(eng):
                for f in p.ops["y"]:
                    f(eng)

    st.close()
    return nc


_CACHE = {}


def kernel(**inputs):
    inp = {k: np.asarray(v) for k, v in inputs.items()}
    smalls, smeta = build_smalls(inp)
    key = "nc"
    if key not in _CACHE:
        consts = build_consts()
        _CACHE[key] = build_nc(consts, smeta, smalls.shape)
    nc = _CACHE[key]

    f32 = np.float32

    def c_(a):
        return np.ascontiguousarray(np.asarray(a, f32))

    wr1 = c_(inp["L0_attn_Wr1"])
    mrows = np.r_[1024:1028, 1032:1036, 1040:1044]
    srows = np.r_[1028:1032, 1036:1040, 1044:1048]
    wr2 = c_(inp["L0_attn_Wr2"])
    base = {
        "wq": c_(inp["L0_attn_Wq"]), "wk": c_(inp["L0_attn_Wk"]),
        "wv": c_(inp["L0_attn_Wv"]), "wb": c_(inp["L0_attn_Wb"]),
        "wid": c_(inp["L0_attn_Wid"]), "wo": c_(inp["L0_attn_Wo"]),
        "wr1x": c_(wr1[0:1024]), "wr1m": c_(wr1[mrows]), "wr1s": c_(wr1[srows]),
        "wr2c0": c_(wr2[:, 0::3]), "wr2c1": c_(wr2[:, 1::3]),
        "wr2c2": c_(wr2[:, 2::3]),
        "wred": c_(inp["L0_ffn_w_red"]), "wg": c_(inp["L0_ffn_wg"]),
        "w1": c_(inp["L0_ffn_w1"]), "w2": c_(inp["L0_ffn_w2"]),
        "b1r": c_(np.asarray(inp["L0_ffn_b1"], f32).reshape(E, 32, 128)
                  .transpose(0, 2, 1)),
        "b2r": c_(np.asarray(inp["L0_ffn_b2"], f32).reshape(E, 8, 128)
                  .transpose(0, 2, 1)),
        "smalls": smalls,
    }
    x = np.asarray(inp["L0_attn_hidden_states"], f32)
    in_maps = []
    for i in range(8):
        m = dict(base)
        m["x"] = np.ascontiguousarray(x[i])
        in_maps.append(m)
    res = run_bass_kernel_spmd(nc, in_maps, core_ids=list(range(8)))
    out = np.stack([r["out"] for r in res.results], axis=0)
    return out
